# version 30 (final)
# baseline (speedup 1.0000x reference)
import numpy as np
import concourse.bass as bass
import concourse.mybir as mybir
from concourse.bass_utils import run_bass_kernel_spmd

F32 = mybir.dt.float32
BF16 = mybir.dt.bfloat16
I32 = mybir.dt.int32
AF = mybir.ActivationFunctionType
ALU = mybir.AluOpType
AX = mybir.AxisListType

NCORES = 8
T = 2048
NT = T // 128
D = 1024
ND = D // 128
HID = 4096
NH = HID // 128
NE = 8
CH = 256
NCH = T // CH
ENGINES = ("sync", "scalar", "tensor", "vector", "gpsimd")

CAPS = [2048, 1024, 768, 512, 512, 384, 384, 256]
RBASE = [0]
for _c in CAPS[:-1]:
    RBASE.append(RBASE[-1] + _c)
PASSES = [(0, RBASE[0], 1024, True), (0, RBASE[0] + 1024, 1024, False)] + [
    (k, RBASE[k], CAPS[k], True) for k in range(1, 8)]
def nblocks(cap):
    out = []
    o = 0
    for w in (512, 512):
        w = min(w, cap - o)
        if w <= 0:
            break
        if w > 256 and w < 512:
            out.append((o, 256)); o += 256
            if cap - o > 0:
                out.append((o, cap - o)); o = cap
        else:
            out.append((o, w)); o += w
    return out


def build_nc(stop="ALL", skip=()):
    nc = bass.Bass("TRN2", target_bir_lowering=False)

    xh = nc.dram_tensor("x", [T, D], F32, kind="ExternalInput")
    wgh = nc.dram_tensor("wg", [D, D], F32, kind="ExternalInput")
    wvh = nc.dram_tensor("wv", [D, D], F32, kind="ExternalInput")
    wdh = nc.dram_tensor("wd", [D, D], F32, kind="ExternalInput")
    bias3h = nc.dram_tensor("bias3", [128, 3, ND], F32, kind="ExternalInput")
    fwgh = nc.dram_tensor("fwg", [D, NE], F32, kind="ExternalInput")
    fbgh = nc.dram_tensor("fbg", [1, NE], F32, kind="ExternalInput")
    fw1h = nc.dram_tensor("fw1", [NE, D, HID], F32, kind="ExternalInput")
    fb1h = nc.dram_tensor("fb1c", [128, NE, NH], F32, kind="ExternalInput")
    fw2h = nc.dram_tensor("fw2", [NE, HID, D], F32, kind="ExternalInput")
    fb2h = nc.dram_tensor("fb2r", [NE, D], F32, kind="ExternalInput")
    eyeh = nc.dram_tensor("eye", [128, 128], F32, kind="ExternalInput")
    onesh = nc.dram_tensor("ones1", [1, 128], F32, kind="ExternalInput")
    cselh = nc.dram_tensor("csel", [128, NCORES], F32, kind="ExternalInput")
    edesch = nc.dram_tensor("edesc", [128, NE], F32, kind="ExternalInput")
    iotah = nc.dram_tensor("iotatm", [128, NT], I32, kind="ExternalInput")
    outh = nc.dram_tensor("out", [T, D], F32, kind="ExternalOutput")

    y_dram = nc.dram_tensor("y_dram", [T, D], F32)
    cc_in = nc.dram_tensor("cc_in", [1, 16 * 128], F32)
    cc_out = nc.dram_tensor("cc_out", [NCORES, 16 * 128], F32)
    NSLOT = 6144
    xg_dram = nc.dram_tensor("xg_dram", [NSLOT, D], BF16)
    oslot_dram = nc.dram_tensor("oslot_dram", [NSLOT, D], F32)
    gs_dram = nc.dram_tensor("gs_dram", [1, T], F32)
    w1s = [nc.dram_tensor(f"w1s{i}", [D, HID], F32) for i in range(3)]
    w2s = [nc.dram_tensor(f"w2s{i}", [HID, D], F32) for i in range(2)]

    BASE = (nc.sbuf_base + 31) // 32 * 32
    LIMIT = nc.sbuf_top
    def at(off, name, shape, dtype):
        nbytes = int(np.prod(shape[1:])) * mybir.dt.size(dtype)
        assert BASE + off + nbytes <= LIMIT, (name, off, nbytes)
        return nc.alloc_sbuf_tensor_at(name, shape, dtype, offset=BASE + off)

    h_all = at(0, "h_all", [128, ND, T], F32)
    zero4k_b = at(0, "zero4k_b", [128, 4096], BF16)
    y1b_tm = at(0, "y1b_tm", [128, NT, D], BF16)
    h_half = at(0, "h_half", [128, NH, 1024], BF16)
    yo_all = at(0, "yo_all", [128, NT, D], F32)
    cumA = at(65536, "cumA", [128, ND, T], BF16)
    ca_f = at(98304, "ca_f", [128, ND, CH], F32)
    a_b2 = at(106496, "a_b2", [128, 2, CH], F32)
    xs_b = at(110592, "xs_b", [128, 2, CH], F32)
    a_b = at(122880, "a_b", [128, 2, CH], F32)
    sg_scr2 = at(112640, "sg_scr2", [128, 2, CH], F32)
    tv_scr2 = at(116736, "tv_scr2", [128, 2, CH], F32)
    res1 = at(65536, "res1", [128, NT, D], F32)
    gfm = at(65536, "gfm", [128, T], F32)
    mfm = at(73728, "mfm", [128, T], F32)
    cumslot = at(81920, "cumslot", [128, T], F32)
    cm8 = at(90112, "cm8", [128, T], F32)
    w2b = at(65536, "w2b", [128, NH, D], BF16)
    out_all = at(65536, "out_all", [128, NT, D], F32)
    wg_b = at(131072, "wg_b", [128, ND, D], BF16)
    wv_b = at(147456, "wv_b", [128, ND, D], BF16)
    wd_b = at(163840, "wd_b", [128, ND, D], BF16)
    y1_fm = at(131072, "y1_fm", [128, ND, T], BF16)
    y1_scr = at(163840, "y1_scr", [128, 2, D], F32)
    y1b_scr = at(172032, "y1b_scr", [128, 2, D], BF16)
    xgt = at(131072, "xgt", [128, 8, D], BF16)
    xg_fm = at(147456, "xg_fm", [128, ND, 1024], BF16)
    o_scr = at(163840, "o_scr", [128, 4, D], F32)
    sq_scr = at(180224, "sq_scr", [128, D], F32)
    xbuf = at(180224, "xbuf", [128, CH // 128, D], BF16)
    xfm = at(184320, "xfm", [128, ND, 2 * CH], BF16)
    sg_scr = at(192512, "sg_scr", [128, CH], F32)
    tv_scr = at(193536, "tv_scr", [128, CH], F32)
    zeros = at(194560, "zeros", [128, CH], F32)
    Aallr = at(180224, "Aallr", [128, NCORES, ND], F32)
    Hallr = at(180480, "Hallr", [128, NCORES, ND], F32)
    c0_b = at(180736, "c0_b", [128, ND], F32)
    c1_b = at(180768, "c1_b", [128, ND], F32)
    carry = at(180800, "carry", [128, ND], F32)
    cc_sb = at(180832, "cc_sb", [128, 16], F32)
    gl_all = at(180224, "gl_all", [128, NT, NE], F32)
    exf = at(180736, "exf", [128, NT, NE], F32)
    eqf = at(181248, "eqf", [128, NT, NE], F32)
    valf = at(181760, "valf", [128, NT, NE], F32)
    gsel = at(182272, "gsel", [128, NT, NE], F32)
    w1b = at(180224, "w1b", [128, 2, ND, 512], BF16)
    stats = at(196608 - 512, "stats", [128, 7 * NT], F32)

    def st(k):
        return stats[:, k * NT:(k + 1) * NT]

    def stc(k, a):
        return stats[:, k * NT + a:k * NT + a + 1]
    p = [196608]
    def perm(name, shape, dtype):
        h = at(p[0], name, shape, dtype)
        p[0] += (int(np.prod(shape[1:])) * mybir.dt.size(dtype) + 31) // 32 * 32
        return h
    eye_f = perm("eye_f", [128, 128], F32)
    eye_b = perm("eye_b", [128, 128], BF16)
    ones_b = perm("ones_b", [1, 128], BF16)
    bias3 = perm("bias3_sb", [128, 3, ND], F32)
    b1r = perm("b1r_sb", [128, NE, NH], F32)
    fwg_b = perm("fwg_sb", [128, ND, NE], BF16)
    fbg_b = perm("fbg_sb", [1, NE], BF16)
    csel = perm("csel_sb", [128, NCORES], F32)
    edesc = perm("edesc_sb", [128, NE], F32)
    b2_cur = perm("b2_cur", [1, 2, D], BF16)
    b2f_cur = perm("b2f_cur", [1, 2, D], F32)
    gslot_tm = perm("gslot_tm", [128, NT], F32)
    gslot_i = perm("gslot_i", [128, NT], I32)
    top1w = perm("top1w", [128, NT], F32)
    onesf8 = perm("onesf8", [8, 1], F32)
    one1 = perm("one1", [1, 1], F32)
    cr8 = perm("cr8", [1, NE], F32)
    rwork = perm("rwork", [1, NE], F32)
    rmk = perm("rmk", [1, 1], F32)
    reqr = perm("reqr", [1, NE], F32)
    rvalr = perm("rvalr", [1, NE], F32)
    rvmr = perm("rvmr", [1, 1], F32)
    rmskr = perm("rmskr", [1, NE], F32)
    ekr = perm("ekr", [1, NE], F32)
    ek_i = perm("ek_i", [1, NE], I32)
    baser = perm("baser", [1, NE], F32)
    rtmp8 = perm("rtmp8", [1, NE], F32)
    rtmp1 = perm("rtmp1", [1, 1], F32)
    iota8r = perm("iota8r", [1, NE], F32)
    base_col = perm("base_col", [8, 1], F32)
    counts_col = perm("counts_col", [8, 1], F32)
    assert p[0] <= LIMIT - BASE, p[0]

    psA = nc.alloc_psum_tensor("psA", [128, 1024], BF16)
    psB = nc.alloc_psum_tensor("psB", [128, 1024], BF16)
    psC = nc.alloc_psum_tensor("psC", [128, 512], F32)
    psD = nc.alloc_psum_tensor("psD", [128, 512], F32)
    psE = nc.alloc_psum_tensor("psE", [128, 1024], F32)
    psF = nc.alloc_psum_tensor("psF", [128, 1024], F32)

    ops = {e: [] for e in ENGINES}
    cnt = {s: 0 for s in ("sL", "sS", "sP", "sA", "sV", "sC")}
    SEM = {}
    REGS = {}

    def emit(eng, fn):
        ops[eng].append(fn)

    def inc(eng, fn, sem, k=1):
        cnt[sem] += k
        t = cnt[sem]
        emit(eng, lambda e, fn=fn, sem=sem, k=k: fn(e).then_inc(SEM[sem], k))
        return t

    def wait(eng, sem, v):
        if v <= 0:
            return
        emit(eng, lambda e, sem=sem, v=v: e.wait_ge(SEM[sem], v))

    def inc_drain(eng, fn, sem, k=1):
        emit(eng, fn)
        cnt[sem] += k
        t = cnt[sem]
        emit(eng, lambda e, sem=sem, k=k: e.drain().then_inc(SEM[sem], k))
        return t

    def dma(eng, out_ap, in_ap, sem="sL", accum=False):
        if accum:
            return inc(eng, lambda e, o=out_ap, i=in_ap: e.dma_start(out=o, in_=i, accum_op=ALU.add), sem, 16)
        return inc(eng, lambda e, o=out_ap, i=in_ap: e.dma_start(out=o, in_=i), sem, 16)

    l_pre = 0
    dma("gpsimd", eye_f[:, :], eyeh.ap())
    dma("gpsimd", eye_b[:, :], eyeh.ap())
    dma("gpsimd", ones_b[:, :], onesh.ap())
    dma("gpsimd", bias3[:, :, :], bias3h.ap())
    dma("gpsimd", fwg_b[:, :, :], bass.AP(fwgh, 0, [[NE, 128], [128 * NE, ND], [1, NE]]))
    dma("gpsimd", fbg_b[:, :], fbgh.ap())
    dma("gpsimd", csel[:, :], cselh.ap())
    dma("gpsimd", edesc[:, :], edesch.ap())
    l_x0 = dma("gpsimd", xbuf[:, :, :], bass.AP(xh, 0, [[D, 128], [128 * D, CH // 128], [1, D]]))
    dma("gpsimd", wg_b[:, :, 0:128], bass.AP(wgh, 0, [[D, 128], [128 * D, ND], [1, 128]]))
    dma("gpsimd", wv_b[:, :, 0:128], bass.AP(wvh, 0, [[D, 128], [128 * D, ND], [1, 128]]))
    l_pre0 = dma("gpsimd", wd_b[:, :, 0:128], bass.AP(wdh, 0, [[D, 128], [128 * D, ND], [1, 128]]))
    l_pre = 0

    v_zero = inc("vector", lambda e: e.memset(zeros[:, :], 0.0), "sV")
    emit("vector", lambda e: e.memset(zero4k_b[:, :], 0.0))
    emit("vector", lambda e: e.memset(onesf8[:, :], 1.0))
    v_zf = inc_drain("vector", lambda e: e.memset(one1[:, :], 1.0), "sV")
    wait("sync", "sV", v_zf)
    s_zf = 0
    for zi in range(12):
        s_zf = dma("sync", bass.AP(xg_dram, zi * 128 * 4096, [[4096, 128], [1, 4096]]),
                   zero4k_b[:, :], sem="sS")

    pj_g = [psE[:, 0:512], psF[:, 0:512]]
    pj_v = [psE[:, 512:1024], psF[:, 512:1024]]
    pj_d = [psC[:, 0:512], psD[:, 0:512]]
    tr_ps = [psA, psB]

    war = {}

    def wait_war(eng, key):
        if key in war:
            s, v = war[key]
            wait(eng, s, v)

    NA = CH // 128
    wait("vector", "sS", s_zf)
    for jj in range(NCH // 2):
        v_xfm = None
        for m in range(2):
            j = 2 * jj + m
            if j == 0:
                l_x = l_x0
            else:
                wait_war("gpsimd", "xbuf")
                l_x = dma("gpsimd", xbuf[:, :, :],
                          bass.AP(xh, CH * j * D, [[D, 128], [128 * D, NA], [1, D]]))
            if j == 1:
                dma("gpsimd", wg_b[:, :, 128:1024], bass.AP(wgh, 128, [[D, 128], [128 * D, ND], [1, 896]]))
                dma("gpsimd", wv_b[:, :, 128:1024], bass.AP(wvh, 128, [[D, 128], [128 * D, ND], [1, 896]]))
                l_pre = dma("gpsimd", wd_b[:, :, 128:1024], bass.AP(wdh, 128, [[D, 128], [128 * D, ND], [1, 896]]))
            for a in range(NA):
                tp = tr_ps[a % 2]
                wait("tensor", "sL", l_x)
                wait_war("tensor", f"tr{a % 2}")
                def tr_group(e, a=a, tp=tp):
                    for dc in range(ND):
                        ins = e.transpose(tp[:, dc * 128:(dc + 1) * 128],
                                          xbuf[:, a, dc * 128:(dc + 1) * 128], eye_b[:, :])
                    return ins
                t_tr = inc("tensor", tr_group, "sP")
                wait("scalar", "sP", t_tr)
                wait_war("scalar", "xfm")
                v_xfm = inc("scalar",
                            lambda e, a=a, m=m, tp=tp: e.activation(
                                out=xfm[:, :, (2 * m + a) * 128:(2 * m + a + 1) * 128],
                                in_=tp[:, :].rearrange("p (k i) -> p k i", i=128),
                                func=AF.Identity), "sA")
                war[f"tr{a % 2}"] = ("sA", v_xfm)
            war["xbuf"] = ("sP", cnt["sP"])

        t_pj_last = 0
        for ec in range(ND):
            par = ec % 2
            for (wsb, pj, wkey) in ((wg_b, pj_g, "g"), (wv_b, pj_v, "v"), (wd_b, pj_d, "d")):
                wait("tensor", "sA", v_xfm)
                wait("tensor", "sL", l_pre0 if (jj == 0 and ec == 0) else l_pre)
                wait_war("tensor", f"pj_{wkey}{par}")
                def mm_group(e, wsb=wsb, pj=pj, par=par, ec=ec):
                    for dk in range(ND):
                        ins = e.matmul(pj[par], wsb[:, dk, ec * 128:(ec + 1) * 128],
                                       xfm[:, dk, :], start=(dk == 0), stop=(dk == ND - 1))
                    return ins
                tv = inc("tensor", mm_group, "sP")
                if wkey == "d":
                    t_pj_last = tv
                war.setdefault("_t_pj", {})
                war["_t_pj"][(wkey, par)] = tv
            tpj = war["_t_pj"]
            a_sg = a_tv = a_sa = 0
            for m in range(2):
                j = 2 * jj + m
                sl = slice(m * CH, (m + 1) * CH)
                wait("scalar", "sP", tpj[("g", par)])
                wait_war("scalar", f"sg{m}")
                a_sg = inc("scalar", lambda e, par=par, ec=ec, sl=sl, m=m: e.activation(
                    out=sg_scr2[:, m, :], in_=pj_g[par][:, sl], func=AF.Sigmoid,
                    bias=bias3[:, 0, ec:ec + 1]), "sA")
                wait("scalar", "sP", tpj[("v", par)])
                wait_war("scalar", f"tv{m}")
                a_tv = inc("scalar", lambda e, par=par, ec=ec, sl=sl, m=m: e.activation(
                    out=tv_scr2[:, m, :], in_=pj_v[par][:, sl], func=AF.Tanh,
                    bias=bias3[:, 1, ec:ec + 1]), "sA")
                ab = a_b if m == 0 else a_b2
                ep = ec % 2
                wait("scalar", "sP", tpj[("d", par)])
                wait_war("scalar", f"a_b{ep}m{m}")
                a_sa = inc("scalar", lambda e, par=par, ec=ec, sl=sl, ab=ab, ep=ep: e.activation(
                    out=ab[:, ep, :], in_=pj_d[par][:, sl], func=AF.Sigmoid,
                    bias=bias3[:, 2, ec:ec + 1]), "sA")
                wait("vector", "sA", a_tv)
                wait_war("vector", f"xs_b{ep}")
                v_xs = inc("vector", lambda e, ep=ep, m=m: e.tensor_mul(
                    xs_b[:, ep, :], sg_scr2[:, m, :], tv_scr2[:, m, :]), "sV")
                war[f"sg{m}"] = ("sV", v_xs)
                war[f"tv{m}"] = ("sV", v_xs)
                wait("vector", "sA", a_sa)
                v_a = inc("vector", lambda e, ep=ep, ab=ab: e.tensor_scalar(
                    out=ab[:, ep, :], in0=ab[:, ep, :], scalar1=0.998, scalar2=0.001,
                    op0=ALU.mult, op1=ALU.add), "sV")
                if j == 0:
                    h_init, c_init = 0.0, 1.0
                else:
                    h_init = h_all[:, ec, CH * j - 1:CH * j]
                    c_init = ca_f[:, ec, CH - 1:CH]
                v_h = inc("vector", lambda e, ec=ec, j=j, h_init=h_init, ab=ab, ep=ep: e.tensor_tensor_scan(
                    out=h_all[:, ec, CH * j:CH * (j + 1)], data0=ab[:, ep, :], data1=xs_b[:, ep, :],
                    initial=h_init, op0=ALU.mult, op1=ALU.add), "sV")
                war[f"xs_b{ep}"] = ("sV", v_h)
                v_ca = inc("vector", lambda e, ec=ec, j=j, c_init=c_init, ab=ab, ep=ep: e.tensor_tensor_scan(
                    out=ca_f[:, ec, :], data0=ab[:, ep, :], data1=zeros[:, :],
                    initial=c_init, op0=ALU.mult, op1=ALU.add), "sV")
                war[f"a_b{ep}m{m}"] = ("sV", v_ca)
                v_cc = inc("vector", lambda e, ec=ec, j=j: e.tensor_copy(
                    cumA[:, ec, CH * j:CH * (j + 1)], ca_f[:, ec, :]), "sV")
            war[f"pj_g{par}"] = ("sA", a_sg)
            war[f"pj_v{par}"] = ("sA", a_tv)
            war[f"pj_d{par}"] = ("sA", a_sa)
        war["xfm"] = ("sP", t_pj_last)

    def finish_early():
        wait("sync", "sV", cnt["sV"])
        wait("sync", "sA", cnt["sA"])
        wait("sync", "sP", cnt["sP"])
        wait("sync", "sL", cnt["sL"])
        s_fin = inc("sync", lambda e: e.dma_start(
            out=bass.AP(outh, 0, [[D, 128], [128 * D, NT], [1, D]]),
            in_=out_all[:, :, :]), "sS", 16)
        wait("sync", "sS", s_fin)

    def emit_all():
        with (
            nc.semaphore("sL") as hL,
            nc.semaphore("sS") as hS,
            nc.semaphore("sP") as hP,
            nc.semaphore("sA") as hA,
            nc.semaphore("sV") as hV,
            nc.semaphore("sC") as hC,
            nc.gpsimd.register("rA") as rA,
            nc.gpsimd.register("rB") as rB,
            nc.sync.register("rS") as rS,
            nc.scalar.register("rC") as rC,
            nc.Block() as block,
        ):
            SEM.update(sL=hL, sS=hS, sP=hP, sA=hA, sV=hV, sC=hC)
            REGS.update(rA=rA, rB=rB, rS=rS, rC=rC)
            for eng in ENGINES:
                getattr(block, eng)(lambda e, lst=ops[eng]: [f(e) for f in lst])

    if stop == "M":
        finish_early(); emit_all(); return nc

    emit("vector", lambda e: e.tensor_copy(cc_sb[:, 0:ND], ca_f[:, :, CH - 1]))
    emit("vector", lambda e: e.drain())
    v_ccsb = inc_drain("vector", lambda e: e.tensor_copy(cc_sb[:, ND:2 * ND], h_all[:, :, T - 1]), "sV")
    wait("gpsimd", "sV", v_ccsb)
    l_cc1 = dma("gpsimd", bass.AP(cc_in, 0, [[16, 128], [1, 16]]), cc_sb[:, :])
    wait("gpsimd", "sL", l_cc1)
    c_cc = inc("gpsimd", lambda e: e.collective_compute(
        "AllGather", ALU.bypass, replica_groups=[list(range(NCORES))],
        ins=[cc_in.ap().opt()], outs=[cc_out.ap().opt()]), "sC", 1)
    wait("gpsimd", "sC", c_cc)
    l_al = dma("gpsimd", Aallr[:, :, :], bass.AP(cc_out, 0, [[16, 128], [16 * 128, NCORES], [1, ND]]))
    l_hl = dma("gpsimd", Hallr[:, :, :], bass.AP(cc_out, ND, [[16, 128], [16 * 128, NCORES], [1, ND]]))
    wait("vector", "sL", l_hl)
    emit("vector", lambda e: e.memset(c0_b[:, :], 0.0))
    emit("vector", lambda e: e.drain())
    emit("vector", lambda e: e.memset(carry[:, :], 0.0))
    emit("vector", lambda e: e.drain())
    for r in range(NCORES):
        src = c0_b if r % 2 == 0 else c1_b
        dst = c1_b if r % 2 == 0 else c0_b
        emit("vector", lambda e, r=r, src=src: e.scalar_tensor_tensor(
            out=carry[:, :], in0=src[:, :], scalar=csel[:, r:r + 1], in1=carry[:, :],
            op0=ALU.mult, op1=ALU.add))
        emit("vector", lambda e: e.drain())
        if r < NCORES - 1:
            emit("vector", lambda e, r=r, src=src, dst=dst: e.tensor_mul(
                dst[:, :], Aallr[:, r, :], src[:, :]))
            emit("vector", lambda e: e.drain())
            emit("vector", lambda e, r=r, dst=dst: e.tensor_add(
                dst[:, :], dst[:, :], Hallr[:, r, :]))
            emit("vector", lambda e: e.drain())
    v_corr = 0
    for ec in range(ND):
        v_corr = inc("vector", lambda e, ec=ec: e.scalar_tensor_tensor(
            out=h_all[:, ec, :], in0=cumA[:, ec, :], scalar=carry[:, ec:ec + 1],
            in1=h_all[:, ec, :], op0=ALU.mult, op1=ALU.add), "sV")

    if stop == "C":
        finish_early(); emit_all(); return nc

    hp = [psE, psF]
    v_res_last = 0
    for a in range(NT):
        tp = hp[a % 2]
        wait("tensor", "sV", v_corr)
        wait_war("tensor", f"hp{a % 2}")
        def htr_group(e, a=a, tp=tp):
            for ec in range(ND):
                ins = e.transpose(tp[:, ec * 128:(ec + 1) * 128],
                                  h_all[:, ec, a * 128:(a + 1) * 128], eye_f[:, :])
            return ins
        t_h = inc("tensor", htr_group, "sP")
        wait("vector", "sP", t_h)
        v_res = inc("vector", lambda e, a=a, tp=tp: e.tensor_copy(res1[:, a, :], tp[:, :]), "sV")
        war[f"hp{a % 2}"] = ("sV", v_res)
        v_res_last = v_res
    wait("gpsimd", "sV", v_res_last)
    l_resx = dma("gpsimd", res1[:, :, :],
                 bass.AP(xh, 0, [[D, 128], [128 * D, NT], [1, D]]), accum=True)
    a_sq = {}
    v_st = {}
    for a in range(NT):
        wait("vector", "sL", l_resx)
        v_st[a] = inc("vector", lambda e, a=a: e.reduce_sum(
            stc(0, a), res1[:, a, :], axis=AX.X), "sV")
        wait("scalar", "sL", l_resx)
        a_sq[a] = inc("scalar", lambda e, a=a: e.activation(
            out=sq_scr[:, :], in_=res1[:, a, :], func=AF.Square,
            accum_out=stc(1, a)), "sA")
    wait("vector", "sA", a_sq[NT - 1])
    emit("vector", lambda e: e.drain())
    emit("vector", lambda e: e.tensor_scalar_mul(st(2), st(0), 1.0 / D))
    emit("vector", lambda e: e.drain())
    emit("vector", lambda e: e.tensor_scalar_mul(st(3), st(1), 1.0 / D))
    emit("vector", lambda e: e.drain())
    emit("vector", lambda e: e.tensor_mul(st(4), st(2), st(2)))
    emit("vector", lambda e: e.drain())
    emit("vector", lambda e: e.tensor_sub(st(3), st(3), st(4)))
    emit("vector", lambda e: e.drain())
    v_var = inc_drain("vector", lambda e: e.tensor_scalar_add(st(3), st(3), 1e-5), "sV")
    wait("scalar", "sV", v_var)
    a_sd = inc_drain("scalar", lambda e: e.activation(
        out=st(4), in_=st(3), func=AF.Sqrt), "sA")
    wait("vector", "sA", a_sd)
    emit("vector", lambda e: e.reciprocal(st(5), st(4)))
    v_rstd = inc_drain("vector", lambda e: e.engine_nop(), "sV")
    v_y1fm_last = 0
    s_y1big = 0
    for a in range(NT):
        par = a % 2
        wait("vector", "sV", v_rstd)
        wait_war("vector", f"y1s{par}")
        v_y1 = inc("vector", lambda e, a=a, par=par: e.tensor_scalar(
            out=y1_scr[:, par, :], in0=res1[:, a, :], scalar1=stc(2, a),
            scalar2=stc(5, a), op0=ALU.subtract, op1=ALU.mult), "sV")
        wait("sync", "sV", v_y1)
        s_y1big = inc("sync", lambda e, a=a, par=par: e.dma_start(
            out=bass.AP(y_dram, a * 128 * D, [[D, 128], [1, D]]),
            in_=y1_scr[:, par, :]), "sS", 16)
        war[f"y1s{par}"] = ("sS", s_y1big)
        v_y1b = inc("vector", lambda e, a=a: e.tensor_scalar(
            out=y1b_tm[:, a, :], in0=res1[:, a, :], scalar1=stc(2, a),
            scalar2=stc(5, a), op0=ALU.subtract, op1=ALU.mult), "sV")
        tp = tr_ps[par]
        wait("tensor", "sV", v_y1b)
        wait_war("tensor", f"tr{par}")
        def ytr_group(e, a=a, tp=tp):
            for dc in range(ND):
                ins = e.transpose(tp[:, dc * 128:(dc + 1) * 128],
                                  y1b_tm[:, a, dc * 128:(dc + 1) * 128], eye_b[:, :])
            return ins
        t_y = inc("tensor", ytr_group, "sP")
        wait("vector", "sP", t_y)
        v_yfm = inc("vector", lambda e, a=a, tp=tp: e.tensor_copy(
            y1_fm[:, :, a * 128:(a + 1) * 128],
            tp[:, :].rearrange("p (k i) -> p k i", i=128)), "sV")
        war[f"tr{par}"] = ("sV", v_yfm)
        v_y1fm_last = v_yfm
        gps = [psC, psD]
        wait("tensor", "sV", v_yfm)
        wait("tensor", "sL", l_pre)
        wait_war("tensor", f"g{par}")
        def gate_group(e, a=a, par=par):
            for dk in range(ND):
                e.matmul(gps[par][:, 0:NE], y1_fm[:, dk, a * 128:(a + 1) * 128],
                         fwg_b[:, dk, :], start=(dk == 0), stop=False)
            return e.matmul(gps[par][:, 0:NE], ones_b[0:1, :], fbg_b[0:1, :],
                            start=False, stop=True)
        t_g = inc("tensor", gate_group, "sP")
        wait("vector", "sP", t_g)
        v_g = inc("vector", lambda e, a=a, par=par: e.tensor_copy(
            gl_all[:, a, :], gps[par][:, 0:NE]), "sV")
        war[f"g{par}"] = ("sV", v_g)

    if stop == "R":
        finish_early(); emit_all(); return nc


    emit("vector", lambda e: e.drain())
    emit("vector", lambda e: e.reduce_max(top1w[:, :].unsqueeze(2), gl_all[:, :, :], axis=AX.X))
    emit("vector", lambda e: e.drain())
    mx_b = top1w[:, :].unsqueeze(2).broadcast_to([128, NT, NE])
    emit("vector", lambda e: e.tensor_tensor(exf[:, :, :], gl_all[:, :, :], mx_b, ALU.subtract))
    v_sub = inc("vector", lambda e: e.tensor_tensor(
        eqf[:, :, :], gl_all[:, :, :], mx_b, ALU.is_ge), "sV")
    wait("scalar", "sV", v_sub)
    a_exp = inc("scalar", lambda e: e.activation(out=exf[:, :, :], in_=exf[:, :, :], func=AF.Exp), "sA")
    wait("vector", "sA", a_exp)
    emit("vector", lambda e: e.reduce_sum(st(0).unsqueeze(2), exf[:, :, :], axis=AX.X))
    emit("vector", lambda e: e.drain())
    emit("vector", lambda e: e.reciprocal(top1w[:, :], st(0)))
    emit("vector", lambda e: e.drain())
    ed_b = edesc[:, :].unsqueeze(1).broadcast_to([128, NT, NE])
    emit("vector", lambda e: e.tensor_tensor(valf[:, :, :], eqf[:, :, :], ed_b, ALU.mult))
    emit("vector", lambda e: e.drain())
    emit("vector", lambda e: e.reduce_max(st(0).unsqueeze(2), valf[:, :, :], axis=AX.X))
    emit("vector", lambda e: e.drain())
    vm_b = st(0).unsqueeze(2).broadcast_to([128, NT, NE])
    emit("vector", lambda e: e.tensor_tensor(gsel[:, :, :], valf[:, :, :], vm_b, ALU.is_equal))
    emit("vector", lambda e: e.drain())
    w_b = top1w[:, :].unsqueeze(2).broadcast_to([128, NT, NE])
    v_gsel = inc("vector", lambda e: e.tensor_tensor(
        gsel[:, :, :], gsel[:, :, :], w_b, ALU.mult), "sV")

    if stop == "G":
        finish_early(); emit_all(); return nc

    t_rt = 0
    for a in range(NT):
        tp = psE if a < 8 else psF
        wait("tensor", "sV", v_gsel)
        if a == 0:
            wait_war("tensor", "hp0")
        if a == 8:
            wait_war("tensor", "hp1")
        def rt_group(e, a=a, tp=tp):
            return e.transpose(tp[0:8, (a % 8) * 128:((a % 8) + 1) * 128],
                               gsel[:, a, :], eye_f[:, :])
        t_rt = inc("tensor", rt_group, "sP")
    wait("vector", "sP", t_rt)
    emit("vector", lambda e: e.tensor_copy(gfm[0:8, 0:1024], psE[0:8, :]))
    v_gfm = inc("vector", lambda e: e.tensor_copy(gfm[0:8, 1024:2048], psF[0:8, :]), "sV")
    emit("vector", lambda e: e.tensor_single_scalar(out=mfm[0:8, :], in_=gfm[0:8, :],
                                                    scalar=0.0, op=ALU.is_gt))
    emit("vector", lambda e: e.tensor_tensor_scan(
        out=cumslot[0:8, :], data0=mfm[0:8, :], data1=mfm[0:8, :],
        initial=0.0, op0=ALU.add, op1=ALU.max))
    emit("vector", lambda e: e.drain())
    emit("vector", lambda e: e.tensor_copy(counts_col[:, :], cumslot[0:8, T - 1:T]))
    emit("vector", lambda e: e.drain())
    v_counts = inc_drain("vector", lambda e: e.tensor_sub(cumslot[0:8, :], cumslot[0:8, :], mfm[0:8, :]), "sV")
    if stop == "RT1":
        finish_early(); emit_all(); return nc
    wait("tensor", "sV", v_counts)
    wait_war("tensor", "g0")
    t_cr = inc("tensor", lambda e: e.matmul(psC[0:1, 0:NE], counts_col[:, 0:1],
                                            eye_f[0:8, 0:NE], start=True, stop=True), "sP")
    wait("vector", "sP", t_cr)
    v_cr = inc_drain("vector", lambda e: e.tensor_copy(cr8[:, :], psC[0:1, 0:NE]), "sV")
    emit("vector", lambda e: e.tensor_scalar(out=iota8r[:, :], in0=edesc[0:1, :],
                                             scalar1=-1.0, scalar2=8.0, op0=ALU.mult, op1=ALU.add))
    emit("vector", lambda e: e.drain())
    emit("vector", lambda e: e.tensor_copy(rwork[:, :], cr8[:, :]))
    emit("vector", lambda e: e.memset(baser[:, :], 0.0))
    emit("vector", lambda e: e.drain())
    for k in range(NE):
        emit("vector", lambda e: e.reduce_max(rmk[:, :], rwork[:, :], axis=AX.X))
        emit("vector", lambda e: e.drain())
        emit("vector", lambda e: e.tensor_scalar(out=reqr[:, :], in0=rwork[:, :],
                                                 scalar1=rmk[0:1, 0:1], scalar2=None, op0=ALU.is_ge))
        emit("vector", lambda e: e.drain())
        emit("vector", lambda e: e.tensor_mul(rvalr[:, :], reqr[:, :], edesc[0:1, :]))
        emit("vector", lambda e: e.drain())
        emit("vector", lambda e: e.reduce_max(rvmr[:, :], rvalr[:, :], axis=AX.X))
        emit("vector", lambda e: e.drain())
        emit("vector", lambda e: e.tensor_scalar(out=rmskr[:, :], in0=rvalr[:, :],
                                                 scalar1=rvmr[0:1, 0:1], scalar2=None, op0=ALU.is_equal))
        emit("vector", lambda e: e.drain())
        emit("vector", lambda e: e.tensor_mul(rtmp8[:, :], rmskr[:, :], iota8r[:, :]))
        emit("vector", lambda e: e.drain())
        emit("vector", lambda e: e.reduce_sum(rtmp1[:, :], rtmp8[:, :], axis=AX.X))
        emit("vector", lambda e: e.drain())
        emit("vector", lambda e, k=k: e.tensor_copy(ekr[:, k:k + 1], rtmp1[:, :]))
        emit("vector", lambda e: e.drain())
        emit("vector", lambda e, k=k: e.scalar_tensor_tensor(
            out=baser[:, :], in0=rmskr[:, :], scalar=float(RBASE[k]), in1=baser[:, :],
            op0=ALU.mult, op1=ALU.add))
        emit("vector", lambda e: e.drain())
        emit("vector", lambda e: e.scalar_tensor_tensor(
            out=rwork[:, :], in0=rmskr[:, :], scalar=-1e9, in1=rwork[:, :],
            op0=ALU.mult, op1=ALU.add))
        emit("vector", lambda e: e.drain())
    emit("vector", lambda e: e.tensor_copy(ek_i[:, :], ekr[:, :]))
    v_rank = inc_drain("vector", lambda e: e.engine_nop(), "sV")
    if stop == "RT2":
        finish_early(); emit_all(); return nc
    wait("tensor", "sV", v_rank)
    wait_war("tensor", "g1")
    t_bc = inc("tensor", lambda e: e.matmul(psD[0:8, 0:1], baser[0:1, :], one1[0:1, 0:1],
                                            start=True, stop=True), "sP")
    wait("vector", "sP", t_bc)
    v_bc = inc_drain("vector", lambda e: e.tensor_copy(base_col[:, :], psD[0:8, 0:1]), "sV")
    emit("vector", lambda e: e.tensor_scalar_add(cumslot[0:8, :], cumslot[0:8, :], base_col[:, 0:1]))
    v_cm = inc("vector", lambda e: e.tensor_mul(cm8[0:8, :], cumslot[0:8, :], mfm[0:8, :]), "sV")
    wait("tensor", "sV", v_cm)
    wait("tensor", "sV", v_cr)
    wait("tensor", "sV", v_bc)
    def cl_group(e):
        for a in range(NT):
            ins = e.matmul(psC[:, a:a + 1], cm8[0:8, a * 128:(a + 1) * 128],
                           onesf8[:, 0:1], start=True, stop=True)
        return ins
    t_cl = inc("tensor", cl_group, "sP")
    wait("vector", "sP", t_cl)
    emit("vector", lambda e: e.tensor_copy(gslot_tm[:, :], psC[:, 0:NT]))
    v_gsr = inc_drain("vector", lambda e: e.engine_nop(), "sV")
    war["g0"] = ("sV", v_gsr)
    war["w2b"] = ("sP", cnt["sP"])
    emit("vector", lambda e: e.tensor_copy(gslot_i[:, :], gslot_tm[:, :]))
    v_gsi = inc_drain("vector", lambda e: e.engine_nop(), "sV")
    if stop == "RT3":
        finish_early(); emit_all(); return nc
    wait("sync", "sV", v_rank)
    for k in (() if "b1" in skip else range(NE)):
        def b1g(e, k=k):
            e.reg_load(REGS["rS"], ek_i[0:1, k:k + 1])
            e.reg_mul(REGS["rS"], REGS["rS"], NH)
            return e.dma_start(out=b1r[:, k, :],
                               in_=bass.AP(fb1h, REGS["rS"], [[NE * NH, 128], [1, NH]]))
        inc("sync", b1g, "sS", 16)

    def w1_stage(rank, par_s):
        def stg_s(e, rank=rank, par_s=par_s):
            e.reg_load(REGS["rS"], ek_i[0:1, rank:rank + 1])
            e.reg_mul(REGS["rS"], REGS["rS"], D * HID)
            return e.dma_start(out=bass.AP(w1s[par_s], 0, [[32768, 64], [8192, 4], [1, 8192]]),
                               in_=bass.AP(fw1h, REGS["rS"], [[32768, 64], [8192, 4], [1, 8192]]))
        def stg_c(e, rank=rank, par_s=par_s):
            e.reg_load(REGS["rC"], ek_i[0:1, rank:rank + 1])
            e.reg_mul(REGS["rC"], REGS["rC"], D * HID)
            e.reg_add(REGS["rC"], REGS["rC"], 64 * 32768)
            return e.dma_start(out=bass.AP(w1s[par_s], 64 * 32768, [[32768, 64], [8192, 4], [1, 8192]]),
                               in_=bass.AP(fw1h, REGS["rC"], [[32768, 64], [8192, 4], [1, 8192]]))
        inc("sync", stg_s, "sS", 16)
        return inc("scalar", stg_c, "sS", 16)

    wait("scalar", "sV", v_rank)
    s_stage = {0: (0 if "w1s" in skip else w1_stage(0, 0)),
               1: (0 if "w1s" in skip else w1_stage(1, 1))}
    wait("scalar", "sV", v_rank)
    s_w2s = {}

    def emit_w2stage(i, rk):
        def w2stage(e, rk=rk, i=i):
            e.reg_load(REGS["rC"], ek_i[0:1, rk:rk + 1])
            e.reg_mul(REGS["rC"], REGS["rC"], HID * D)
            return e.dma_start(out=bass.AP(w2s[i], 0, [[32768, 128], [8192, 4], [1, 8192]]),
                               in_=bass.AP(fw2h, REGS["rC"], [[32768, 128], [8192, 4], [1, 8192]]))
        s_w2s[rk] = inc("scalar", w2stage, "sS", 16)
    if stop == "RT4":
        finish_early(); emit_all(); return nc
    wait("gpsimd", "sV", v_gsi)
    wait("gpsimd", "sS", s_zf)
    l_ysc = 0
    for a in range(NT):
        l_ysc = inc("gpsimd", lambda e, a=a: e.indirect_dma_start(
            out=xg_dram.ap(), out_offset=bass.IndirectOffsetOnAxis(gslot_i[:, a:a + 1], 0),
            in_=y1b_tm[:, a, :].opt(), in_offset=None,
            bounds_check=NSLOT - 1, oob_is_err=False), "sL", 16)
    war["y1tm"] = ("sL", l_ysc)
    l_route_done = cnt["sL"]

    if stop == "RT":
        finish_early(); emit_all(); return nc

    p1ps = [psC, psD]
    p2ps = [psE, psF]
    nw2 = 0
    l_w2v = 0
    l_b2 = 0
    s_w1reads = {}
    for pi, (rank, sbase, cap, loadw) in enumerate(PASSES):
        ntp = cap // 128
        par_p = pi % 2
        if "w2s" not in skip:
            if pi == 4:
                emit_w2stage(0, NE - 2)
            elif pi == 5:
                emit_w2stage(1, NE - 1)
        wait("gpsimd", "sL", l_route_done)
        wait_war("gpsimd", "xgt")
        l_xg = dma("gpsimd", xgt[:, 0:ntp, :],
                   bass.AP(xg_dram, sbase * D, [[D, 128], [128 * D, ntp], [1, D]]))
        if loadw:
            nw2 += 1
        b2p = nw2 % 2
        v_xgfm = 0
        for t in range(ntp):
            tp = tr_ps[t % 2]
            wait("tensor", "sL", l_xg)
            wait_war("tensor", f"tr{t % 2}")
            def xtr_group(e, t=t, tp=tp):
                for dc in range(ND):
                    ins = e.transpose(tp[:, dc * 128:(dc + 1) * 128],
                                      xgt[:, t, dc * 128:(dc + 1) * 128], eye_b[:, :])
                return ins
            t_x = inc("tensor", xtr_group, "sP")
            wait("vector", "sP", t_x)
            wait_war("vector", "xg_fm")
            v_xgfm = inc("vector", lambda e, t=t, tp=tp: e.tensor_copy(
                xg_fm[:, :, t * 128:(t + 1) * 128],
                tp[:, :].rearrange("p (k i) -> p k i", i=128)), "sV")
            war[f"tr{t % 2}"] = ("sV", v_xgfm)
        war["xgt"] = ("sP", cnt["sP"])
        nbl = nblocks(cap)
        gi = 0
        t_p1_last = 0
        li_p = nw2 - 1
        wait("gpsimd", "sS", s_stage[li_p])
        for hc in range(ND):
            parw = hc % 2
            wait_war("gpsimd", f"w1b{parw}")
            l_w1 = dma("gpsimd", w1b[:, parw, :, :],
                       bass.AP(w1s[li_p % 3], hc * 4096,
                               [[32768, 128], [1, 4096]]))
            for hk in range(4):
                for (no, nw) in nbl:
                    pp = p1ps[gi % 2]
                    gi += 1
                    wait("tensor", "sL", l_w1)
                    wait("tensor", "sV", v_xgfm)
                    wait_war("tensor", f"p1_{(gi - 1) % 2}")
                    def p1_group(e, hc=hc, hk=hk, no=no, nw=nw, pp=pp, parw=parw):
                        for dk in range(ND):
                            ins = e.matmul(pp[:, 0:nw],
                                           w1b[:, parw, dk, hk * 128:(hk + 1) * 128],
                                           xg_fm[:, dk, no:no + nw],
                                           start=(dk == 0), stop=(dk == ND - 1))
                        return ins
                    t_p1 = inc("tensor", p1_group, "sP")
                    t_p1_last = t_p1
                    wait("scalar", "sP", t_p1)
                    wait_war("scalar", "p2_read")
                    wait_war("scalar", "y1tm")
                    a_h = inc("scalar", lambda e, rank=rank, hc=hc, hk=hk, no=no, nw=nw, pp=pp: e.activation(
                        out=h_half[:, hc * 4 + hk, no:no + nw],
                        in_=pp[:, 0:nw], func=AF.Silu,
                        bias=b1r[:, rank, hc * 4 + hk:hc * 4 + hk + 1]), "sA")
                    war[f"p1_{(gi - 1) % 2}"] = ("sA", a_h)
            war[f"w1b{parw}"] = ("sP", t_p1_last)
        war["xg_fm"] = ("sP", t_p1_last)
        a_p1_done = cnt["sA"]
        s_w1reads[li_p] = cnt["sL"]
        if loadw:
            wait("gpsimd", "sL", l_route_done)
            wait_war("gpsimd", "w2b")
            if rank >= NE - 2:
                wait("gpsimd", "sS", s_w2s[rank])
                l_w2v = dma("gpsimd", w2b[:, :, :],
                            bass.AP(w2s[rank - (NE - 2)], 0, [[32768, 128], [8192, 4], [1, 8192]]))
            else:
                def w2g(e, rank=rank):
                    e.reg_load(REGS["rA"], ek_i[0:1, rank:rank + 1])
                    e.reg_mul(REGS["rA"], REGS["rA"], HID * D)
                    return e.dma_start(out=w2b[:, :, :],
                                       in_=bass.AP(fw2h, REGS["rA"], [[32768, 128], [8192, 4], [1, 8192]]))
                l_w2v = inc("gpsimd", w2g, "sL", 16)
            def b2g(e, rank=rank, b2p=b2p):
                e.reg_load(REGS["rS"], ek_i[0:1, rank:rank + 1])
                e.reg_mul(REGS["rS"], REGS["rS"], D)
                return e.dma_start(out=b2f_cur[:, b2p, :],
                                   in_=bass.AP(fb2h, REGS["rS"], [[1, 1], [1, D]]))
            s_b2 = inc("sync", b2g, "sS", 16)
            wait("vector", "sS", s_b2)
            wait_war("vector", f"b2c{b2p}")
            l_b2 = inc("vector", lambda e, b2p=b2p: e.tensor_copy(
                b2_cur[:, b2p, :], b2f_cur[:, b2p, :]), "sV")
            li = nw2 - 1
            if li + 1 < NE and li + 1 not in s_stage:
                if li - 2 >= 0:
                    wait("sync", "sL", s_w1reads.get(li - 2, 0))
                    wait("scalar", "sL", s_w1reads.get(li - 2, 0))
                s_stage[li + 1] = w1_stage(li + 1, (li + 1) % 3)
        wait("tensor", "sA", a_p1_done)
        wait("tensor", "sV", l_b2)
        wait("tensor", "sL", l_w2v)
        for t in range(ntp):
            pp = p2ps[t % 2]
            wait_war("tensor", f"p2_{t % 2}")
            def p2_group(e, t=t, pp=pp, b2p=nw2 % 2):
                for hk in range(NH):
                    e.matmul(pp[:, 0:512], h_half[:, hk, t * 128:(t + 1) * 128],
                             w2b[:, hk, 0:512], start=(hk == 0), stop=False)
                    e.matmul(pp[:, 512:1024], h_half[:, hk, t * 128:(t + 1) * 128],
                             w2b[:, hk, 512:1024], start=(hk == 0), stop=False)
                e.matmul(pp[:, 0:512], ones_b[0:1, :], b2_cur[0:1, b2p, 0:512],
                         start=False, stop=True)
                return e.matmul(pp[:, 512:1024], ones_b[0:1, :], b2_cur[0:1, b2p, 512:1024],
                                start=False, stop=True)
            t_p2 = inc("tensor", p2_group, "sP")
            par_o = t % 4
            wait("scalar", "sP", t_p2)
            wait_war("scalar", f"o_scr{par_o}")
            a_o = inc("scalar", lambda e, pp=pp, par_o=par_o: e.activation(
                out=o_scr[:, par_o, :], in_=pp[:, :], func=AF.Identity), "sA")
            war[f"p2_{t % 2}"] = ("sA", a_o)
            wait("sync", "sA", a_o)
            s_o = inc("sync", lambda e, par_o=par_o, row0=sbase + t * 128: e.dma_start(
                out=bass.AP(oslot_dram, row0 * D, [[D, 128], [1, D]]),
                in_=o_scr[:, par_o, :]), "sS", 16)
            war[f"o_scr{par_o}"] = ("sS", s_o)
        war["p2_read"] = ("sP", cnt["sP"])
        war["w2b"] = ("sP", cnt["sP"])
        war[f"b2c{nw2 % 2}"] = ("sP", cnt["sP"])
    s_oslot_done = cnt["sS"]
    l_acc_done = cnt["sL"]
    s_y1_done = cnt["sS"]

    if stop == "F":
        finish_early(); emit_all(); return nc

    wait("sync", "sS", s_y1big)
    s_yo = inc("sync", lambda e: e.dma_start(
        out=yo_all[:, :, :], in_=bass.AP(y_dram, 0, [[D, 128], [128 * D, NT], [1, D]])), "sS", 16)
    a_sq2 = 0
    wait("gpsimd", "sS", s_oslot_done)
    for a in range(NT):
        par = a % 4
        wait_war("gpsimd", f"og{par}")
        l_og = inc("gpsimd", lambda e, a=a, par=par: e.indirect_dma_start(
            out=o_scr[:, par, :].opt(), out_offset=None,
            in_=oslot_dram.ap(), in_offset=bass.IndirectOffsetOnAxis(gslot_i[:, a:a + 1], 0),
            bounds_check=NSLOT - 1, oob_is_err=False), "sL", 16)
        wait("vector", "sL", l_og)
        wait("vector", "sS", s_yo)
        v_y = inc("vector", lambda e, a=a, par=par: e.scalar_tensor_tensor(
            out=yo_all[:, a, :], in0=o_scr[:, par, :], scalar=top1w[:, a:a + 1],
            in1=yo_all[:, a, :], op0=ALU.mult, op1=ALU.add), "sV")
        war[f"og{par}"] = ("sV", v_y)
        wait("vector", "sV", v_y)
        inc("vector", lambda e, a=a: e.reduce_sum(
            stc(0, a), yo_all[:, a, :], axis=AX.X), "sV")
        wait("scalar", "sV", v_y)
        a_sq2 = inc("scalar", lambda e, a=a: e.activation(
            out=sq_scr[:, :], in_=yo_all[:, a, :], func=AF.Square,
            accum_out=stc(1, a)), "sA")
    wait("vector", "sA", a_sq2)
    emit("vector", lambda e: e.drain())
    emit("vector", lambda e: e.tensor_scalar_mul(st(2), st(0), 1.0 / D))
    emit("vector", lambda e: e.drain())
    emit("vector", lambda e: e.tensor_scalar_mul(st(3), st(1), 1.0 / D))
    emit("vector", lambda e: e.drain())
    emit("vector", lambda e: e.tensor_mul(st(4), st(2), st(2)))
    emit("vector", lambda e: e.drain())
    emit("vector", lambda e: e.tensor_sub(st(3), st(3), st(4)))
    emit("vector", lambda e: e.drain())
    v_var2 = inc_drain("vector", lambda e: e.tensor_scalar_add(st(3), st(3), 1e-5), "sV")
    wait("scalar", "sV", v_var2)
    a_sd2 = inc_drain("scalar", lambda e: e.activation(
        out=st(4), in_=st(3), func=AF.Sqrt), "sA")
    wait("vector", "sA", a_sd2)
    emit("vector", lambda e: e.reciprocal(st(5), st(4)))
    v_rstd2 = inc_drain("vector", lambda e: e.engine_nop(), "sV")
    s_out = 0
    for a in range(NT):
        wait("vector", "sV", v_rstd2)
        v_out = inc("vector", lambda e, a=a: e.tensor_scalar(
            out=out_all[:, a, :], in0=yo_all[:, a, :], scalar1=stc(2, a),
            scalar2=stc(5, a), op0=ALU.subtract, op1=ALU.mult), "sV")
        wait("sync", "sV", v_out)
        s_out = inc("sync", lambda e, a=a: e.dma_start(
            out=bass.AP(outh, a * 128 * D, [[D, 128], [1, D]]),
            in_=out_all[:, a, :]), "sS", 16)
    wait("sync", "sS", s_out)

    emit_all()
    return nc


_NC_CACHE = None


def kernel(L0_attn_x, L0_attn_Wg, L0_attn_bg, L0_attn_Wv, L0_attn_bv, L0_attn_Wd,
           L0_attn_bd, L0_ffn_Wg, L0_ffn_bg, L0_ffn_W1, L0_ffn_b1, L0_ffn_W2, L0_ffn_b2):
    global _NC_CACHE
    x = np.ascontiguousarray(np.asarray(L0_attn_x, dtype=np.float32).reshape(16384, 1024))
    wg = np.ascontiguousarray(np.asarray(L0_attn_Wg, np.float32))
    wv = np.ascontiguousarray(np.asarray(L0_attn_Wv, np.float32))
    wd = np.ascontiguousarray(np.asarray(L0_attn_Wd, np.float32))
    bias3 = np.ascontiguousarray(
        np.stack([np.asarray(b, np.float32).reshape(8, 128).T for b in
                  (L0_attn_bg, L0_attn_bv, L0_attn_bd)], axis=1))
    fwg = np.ascontiguousarray(np.asarray(L0_ffn_Wg, np.float32))
    fbg = np.ascontiguousarray(np.asarray(L0_ffn_bg, np.float32).reshape(1, 8))
    fw1_raw = np.asarray(L0_ffn_W1, np.float32)
    fw1 = np.ascontiguousarray(
        fw1_raw.reshape(8, 8, 128, 8, 512).transpose(0, 2, 3, 1, 4).reshape(8, 1024, 4096))
    fb1c = np.ascontiguousarray(np.asarray(L0_ffn_b1, np.float32).reshape(8, 32, 128).transpose(2, 0, 1))
    fw2_raw = np.asarray(L0_ffn_W2, np.float32)
    fw2 = np.ascontiguousarray(
        fw2_raw.reshape(8, 32, 128, 1024).transpose(0, 2, 1, 3).reshape(8, 4096, 1024))
    fb2r = np.ascontiguousarray(np.asarray(L0_ffn_b2, np.float32))
    eye = np.eye(128, dtype=np.float32)
    ones1 = np.ones((1, 128), np.float32)
    edesc = np.tile(np.arange(8, 0, -1, dtype=np.float32)[None, :], (128, 1))
    iotatm = (np.arange(16, dtype=np.int32)[None, :] * 128 +
              np.arange(128, dtype=np.int32)[:, None]).astype(np.int32)

    import os as _os
    if _NC_CACHE is None:
        _NC_CACHE = build_nc(stop=_os.environ.get("KSTOP", "ALL"),
                             skip=tuple(_os.environ.get("KSKIP", "").split(",")))
    nc = _NC_CACHE

    in_maps = []
    for c in range(NCORES):
        csel = np.zeros((128, 8), np.float32)
        csel[:, c] = 1.0
        in_maps.append(dict(
            x=np.ascontiguousarray(x[c * 2048:(c + 1) * 2048]),
            wg=wg, wv=wv, wd=wd, bias3=bias3, fwg=fwg, fbg=fbg, fw1=fw1,
            fb1c=fb1c, fw2=fw2, fb2r=fb2r, eye=eye, ones1=ones1,
            csel=csel, edesc=edesc, iotatm=iotatm,
        ))
    res = run_bass_kernel_spmd(nc, in_maps, core_ids=list(range(NCORES)))
    out = np.concatenate([res.results[c]["out"] for c in range(NCORES)], axis=0)
    return out.reshape(1, 16384, 1024)


# version 33
# speedup vs baseline: 1.0173x; 1.0173x over previous
import numpy as np
import concourse.bass as bass
import concourse.mybir as mybir
from concourse.bass_utils import run_bass_kernel_spmd

F32 = mybir.dt.float32
BF16 = mybir.dt.bfloat16
I32 = mybir.dt.int32
AF = mybir.ActivationFunctionType
ALU = mybir.AluOpType
AX = mybir.AxisListType

NCORES = 8
T = 2048
NT = T // 128
D = 1024
ND = D // 128
HID = 4096
NH = HID // 128
NE = 8
CH = 256
NCH = T // CH
ENGINES = ("sync", "scalar", "tensor", "vector", "gpsimd")

CAPS = [2048, 1024, 768, 512, 512, 384, 384, 256]
RBASE = [0]
for _c in CAPS[:-1]:
    RBASE.append(RBASE[-1] + _c)
PASSES = [(0, RBASE[0], 1024, True), (0, RBASE[0] + 1024, 1024, False)] + [
    (k, RBASE[k], CAPS[k], True) for k in range(1, 8)]
def nblocks(cap):
    out = []
    o = 0
    for w in (512, 512):
        w = min(w, cap - o)
        if w <= 0:
            break
        if w > 256 and w < 512:
            out.append((o, 256)); o += 256
            if cap - o > 0:
                out.append((o, cap - o)); o = cap
        else:
            out.append((o, w)); o += w
    return out


def build_nc(stop="ALL", skip=()):
    nc = bass.Bass("TRN2", target_bir_lowering=False)

    xh = nc.dram_tensor("x", [T, D], F32, kind="ExternalInput")
    wgh = nc.dram_tensor("wg", [D, D], F32, kind="ExternalInput")
    wvh = nc.dram_tensor("wv", [D, D], F32, kind="ExternalInput")
    wdh = nc.dram_tensor("wd", [D, D], F32, kind="ExternalInput")
    bias3h = nc.dram_tensor("bias3", [128, 3, ND], F32, kind="ExternalInput")
    fwgh = nc.dram_tensor("fwg", [D, NE], F32, kind="ExternalInput")
    fbgh = nc.dram_tensor("fbg", [1, NE], F32, kind="ExternalInput")
    fw1h = nc.dram_tensor("fw1", [NE, D, HID], F32, kind="ExternalInput")
    fb1h = nc.dram_tensor("fb1c", [128, NE, NH], F32, kind="ExternalInput")
    fw2h = nc.dram_tensor("fw2", [NE, HID, D], F32, kind="ExternalInput")
    fb2h = nc.dram_tensor("fb2r", [NE, D], F32, kind="ExternalInput")
    eyeh = nc.dram_tensor("eye", [128, 128], F32, kind="ExternalInput")
    onesh = nc.dram_tensor("ones1", [1, 128], F32, kind="ExternalInput")
    cselh = nc.dram_tensor("csel", [128, NCORES], F32, kind="ExternalInput")
    edesch = nc.dram_tensor("edesc", [128, NE], F32, kind="ExternalInput")
    iotah = nc.dram_tensor("iotatm", [128, NT], I32, kind="ExternalInput")
    outh = nc.dram_tensor("out", [T, D], F32, kind="ExternalOutput")

    y_dram = nc.dram_tensor("y_dram", [T, D], F32)
    cc_in = nc.dram_tensor("cc_in", [1, 16 * 128], F32)
    cc_out = nc.dram_tensor("cc_out", [NCORES, 16 * 128], F32)
    NSLOT = 6144
    xg_dram = nc.dram_tensor("xg_dram", [NSLOT, D], BF16)
    oslot_dram = nc.dram_tensor("oslot_dram", [NSLOT, D], F32)
    gs_dram = nc.dram_tensor("gs_dram", [1, T], F32)
    w1s = [nc.dram_tensor(f"w1s{i}", [D, HID], F32) for i in range(3)]
    w2s = [nc.dram_tensor(f"w2s{i}", [HID, D], F32) for i in range(2)]

    BASE = (nc.sbuf_base + 31) // 32 * 32
    LIMIT = nc.sbuf_top
    def at(off, name, shape, dtype):
        nbytes = int(np.prod(shape[1:])) * mybir.dt.size(dtype)
        assert BASE + off + nbytes <= LIMIT, (name, off, nbytes)
        return nc.alloc_sbuf_tensor_at(name, shape, dtype, offset=BASE + off)

    h_all = at(0, "h_all", [128, ND, T], F32)
    zero4k_b = at(0, "zero4k_b", [128, 4096], BF16)
    y1b_tm = at(0, "y1b_tm", [128, NT, D], BF16)
    h_half = at(0, "h_half", [128, NH, 1024], BF16)
    yo_all = at(0, "yo_all", [128, NT, D], F32)
    cumA = at(65536, "cumA", [128, ND, T], BF16)
    ca_f = at(98304, "ca_f", [128, ND, CH], F32)
    a_b2 = at(106496, "a_b2", [128, 2, CH], F32)
    xs_b = at(110592, "xs_b", [128, 2, CH], F32)
    a_b = at(126976, "a_b", [128, 2, CH], F32)
    sg_scr2 = at(112640, "sg_scr2", [128, 2, CH], F32)
    tv_scr2 = at(116736, "tv_scr2", [128, 2, CH], F32)
    xfm_b = at(118784, "xfm_b", [128, ND, 2 * CH], BF16)
    res1 = at(65536, "res1", [128, NT, D], F32)
    gfm = at(65536, "gfm", [128, T], F32)
    mfm = at(73728, "mfm", [128, T], F32)
    cumslot = at(81920, "cumslot", [128, T], F32)
    cm8 = at(90112, "cm8", [128, T], F32)
    w2b = at(65536, "w2b", [128, NH, D], BF16)
    out_all = at(65536, "out_all", [128, NT, D], F32)
    wg_b = at(131072, "wg_b", [128, ND, D], BF16)
    wv_b = at(147456, "wv_b", [128, ND, D], BF16)
    wd_b = at(163840, "wd_b", [128, ND, D], BF16)
    y1_fm = at(131072, "y1_fm", [128, ND, T], BF16)
    y1_scr = at(163840, "y1_scr", [128, 2, D], F32)
    y1b_scr = at(172032, "y1b_scr", [128, 2, D], BF16)
    xgt = at(131072, "xgt", [128, 8, D], BF16)
    xg_fm = at(147456, "xg_fm", [128, ND, 1024], BF16)
    o_scr = at(163840, "o_scr", [128, 4, D], F32)
    sq_scr = at(180224, "sq_scr", [128, D], F32)
    xbuf = at(180224, "xbuf", [128, CH // 128, D], BF16)
    xfm = at(184320, "xfm", [128, ND, 2 * CH], BF16)
    sg_scr = at(192512, "sg_scr", [128, CH], F32)
    tv_scr = at(193536, "tv_scr", [128, CH], F32)
    zeros = at(194560, "zeros", [128, CH], F32)
    Aallr = at(180224, "Aallr", [128, NCORES, ND], F32)
    Hallr = at(180480, "Hallr", [128, NCORES, ND], F32)
    c0_b = at(180736, "c0_b", [128, ND], F32)
    c1_b = at(180768, "c1_b", [128, ND], F32)
    carry = at(180800, "carry", [128, ND], F32)
    cc_sb = at(180832, "cc_sb", [128, 16], F32)
    gl_all = at(180224, "gl_all", [128, NT, NE], F32)
    exf = at(180736, "exf", [128, NT, NE], F32)
    eqf = at(181248, "eqf", [128, NT, NE], F32)
    valf = at(181760, "valf", [128, NT, NE], F32)
    gsel = at(182272, "gsel", [128, NT, NE], F32)
    w1b = at(180224, "w1b", [128, 2, ND, 512], BF16)
    stats = at(196608 - 512, "stats", [128, 7 * NT], F32)

    def st(k):
        return stats[:, k * NT:(k + 1) * NT]

    def stc(k, a):
        return stats[:, k * NT + a:k * NT + a + 1]
    p = [196608]
    def perm(name, shape, dtype):
        h = at(p[0], name, shape, dtype)
        p[0] += (int(np.prod(shape[1:])) * mybir.dt.size(dtype) + 31) // 32 * 32
        return h
    eye_f = perm("eye_f", [128, 128], F32)
    eye_b = perm("eye_b", [128, 128], BF16)
    ones_b = perm("ones_b", [1, 128], BF16)
    bias3 = perm("bias3_sb", [128, 3, ND], F32)
    b1r = perm("b1r_sb", [128, NE, NH], F32)
    fwg_b = perm("fwg_sb", [128, ND, NE], BF16)
    fbg_b = perm("fbg_sb", [1, NE], BF16)
    csel = perm("csel_sb", [128, NCORES], F32)
    edesc = perm("edesc_sb", [128, NE], F32)
    b2_cur = perm("b2_cur", [1, 2, D], BF16)
    b2f_cur = perm("b2f_cur", [1, 2, D], F32)
    gslot_tm = perm("gslot_tm", [128, NT], F32)
    gslot_i = perm("gslot_i", [128, NT], I32)
    top1w = perm("top1w", [128, NT], F32)
    onesf8 = perm("onesf8", [8, 1], F32)
    one1 = perm("one1", [1, 1], F32)
    cr8 = perm("cr8", [1, NE], F32)
    rwork = perm("rwork", [1, NE], F32)
    rmk = perm("rmk", [1, 1], F32)
    reqr = perm("reqr", [1, NE], F32)
    rvalr = perm("rvalr", [1, NE], F32)
    rvmr = perm("rvmr", [1, 1], F32)
    rmskr = perm("rmskr", [1, NE], F32)
    ekr = perm("ekr", [1, NE], F32)
    ek_i = perm("ek_i", [1, NE], I32)
    baser = perm("baser", [1, NE], F32)
    rtmp8 = perm("rtmp8", [1, NE], F32)
    rtmp1 = perm("rtmp1", [1, 1], F32)
    iota8r = perm("iota8r", [1, NE], F32)
    base_col = perm("base_col", [8, 1], F32)
    counts_col = perm("counts_col", [8, 1], F32)
    assert p[0] <= LIMIT - BASE, p[0]

    psA = nc.alloc_psum_tensor("psA", [128, 1024], BF16)
    psB = nc.alloc_psum_tensor("psB", [128, 1024], BF16)
    psC = nc.alloc_psum_tensor("psC", [128, 512], F32)
    psD = nc.alloc_psum_tensor("psD", [128, 512], F32)
    psE = nc.alloc_psum_tensor("psE", [128, 1024], F32)
    psF = nc.alloc_psum_tensor("psF", [128, 1024], F32)

    ops = {e: [] for e in ENGINES}
    cnt = {s: 0 for s in ("sL", "sS", "sP", "sA", "sV", "sC")}
    SEM = {}
    REGS = {}

    def emit(eng, fn):
        ops[eng].append(fn)

    def inc(eng, fn, sem, k=1):
        cnt[sem] += k
        t = cnt[sem]
        emit(eng, lambda e, fn=fn, sem=sem, k=k: fn(e).then_inc(SEM[sem], k))
        return t

    def wait(eng, sem, v):
        if v <= 0:
            return
        emit(eng, lambda e, sem=sem, v=v: e.wait_ge(SEM[sem], v))

    def inc_drain(eng, fn, sem, k=1):
        emit(eng, fn)
        cnt[sem] += k
        t = cnt[sem]
        emit(eng, lambda e, sem=sem, k=k: e.drain().then_inc(SEM[sem], k))
        return t

    def dma(eng, out_ap, in_ap, sem="sL", accum=False):
        if accum:
            return inc(eng, lambda e, o=out_ap, i=in_ap: e.dma_start(out=o, in_=i, accum_op=ALU.add), sem, 16)
        return inc(eng, lambda e, o=out_ap, i=in_ap: e.dma_start(out=o, in_=i), sem, 16)

    l_pre = 0
    dma("gpsimd", eye_f[:, :], eyeh.ap())
    dma("gpsimd", eye_b[:, :], eyeh.ap())
    dma("gpsimd", ones_b[:, :], onesh.ap())
    dma("gpsimd", bias3[:, :, :], bias3h.ap())
    dma("gpsimd", fwg_b[:, :, :], bass.AP(fwgh, 0, [[NE, 128], [128 * NE, ND], [1, NE]]))
    dma("gpsimd", fbg_b[:, :], fbgh.ap())
    dma("gpsimd", csel[:, :], cselh.ap())
    dma("gpsimd", edesc[:, :], edesch.ap())
    l_x0 = dma("gpsimd", xbuf[:, :, :], bass.AP(xh, 0, [[D, 128], [128 * D, CH // 128], [1, D]]))
    dma("gpsimd", wg_b[:, :, 0:128], bass.AP(wgh, 0, [[D, 128], [128 * D, ND], [1, 128]]))
    dma("gpsimd", wv_b[:, :, 0:128], bass.AP(wvh, 0, [[D, 128], [128 * D, ND], [1, 128]]))
    l_pre0 = dma("gpsimd", wd_b[:, :, 0:128], bass.AP(wdh, 0, [[D, 128], [128 * D, ND], [1, 128]]))
    l_pre = 0

    v_zero = inc("vector", lambda e: e.memset(zeros[:, :], 0.0), "sV")
    emit("vector", lambda e: e.memset(zero4k_b[:, :], 0.0))
    emit("vector", lambda e: e.memset(onesf8[:, :], 1.0))
    v_zf = inc_drain("vector", lambda e: e.memset(one1[:, :], 1.0), "sV")
    wait("sync", "sV", v_zf)
    s_zf = 0
    for zi in range(12):
        s_zf = dma("sync", bass.AP(xg_dram, zi * 128 * 4096, [[4096, 128], [1, 4096]]),
                   zero4k_b[:, :], sem="sS")

    pj_g = [psE[:, 0:512], psF[:, 0:512]]
    pj_v = [psE[:, 512:1024], psF[:, 512:1024]]
    pj_d = [psC[:, 0:512], psD[:, 0:512]]
    tr_ps = [psA, psB]

    war = {}

    def wait_war(eng, key):
        if key in war:
            s, v = war[key]
            wait(eng, s, v)

    NA = CH // 128
    wait("vector", "sS", s_zf)
    def emit_group_input(jj):
        xf = xfm if jj % 2 == 0 else xfm_b
        v_last = 0
        for m in range(2):
            j = 2 * jj + m
            if j == 0:
                l_x = l_x0
            else:
                wait_war("gpsimd", "xbuf")
                l_x = dma("gpsimd", xbuf[:, :, :],
                          bass.AP(xh, CH * j * D, [[D, 128], [128 * D, NA], [1, D]]))
            if j == 1:
                dma("gpsimd", wg_b[:, :, 128:1024], bass.AP(wgh, 128, [[D, 128], [128 * D, ND], [1, 896]]))
                dma("gpsimd", wv_b[:, :, 128:1024], bass.AP(wvh, 128, [[D, 128], [128 * D, ND], [1, 896]]))
                lp = dma("gpsimd", wd_b[:, :, 128:1024], bass.AP(wdh, 128, [[D, 128], [128 * D, ND], [1, 896]]))
                l_pre_box[0] = lp
            for a in range(NA):
                tp = tr_ps[a % 2]
                wait("tensor", "sL", l_x)
                wait_war("tensor", f"tr{a % 2}")
                def tr_group(e, a=a, tp=tp):
                    for dc in range(ND):
                        ins = e.transpose(tp[:, dc * 128:(dc + 1) * 128],
                                          xbuf[:, a, dc * 128:(dc + 1) * 128], eye_b[:, :])
                    return ins
                t_tr = inc("tensor", tr_group, "sP")
                wait("scalar", "sP", t_tr)
                wait_war("scalar", f"xfm{jj % 2}")
                v_last = inc("scalar",
                             lambda e, a=a, m=m, tp=tp, xf=xf: e.activation(
                                 out=xf[:, :, (2 * m + a) * 128:(2 * m + a + 1) * 128],
                                 in_=tp[:, :].rearrange("p (k i) -> p k i", i=128),
                                 func=AF.Identity), "sA")
                war[f"tr{a % 2}"] = ("sA", v_last)
            war["xbuf"] = ("sP", cnt["sP"])
        return v_last

    l_pre_box = [0]
    v_xfm_g = {0: emit_group_input(0)}
    for jj in range(NCH // 2):
        xf = xfm if jj % 2 == 0 else xfm_b
        v_xfm = v_xfm_g[jj]
        t_pj_last = 0
        for ec in range(ND):
            par = ec % 2
            for (wsb, pj, wkey) in ((wg_b, pj_g, "g"), (wv_b, pj_v, "v"), (wd_b, pj_d, "d")):
                wait("tensor", "sA", v_xfm)
                wait("tensor", "sL", l_pre0 if (jj == 0 and ec == 0) else l_pre_box[0])
                wait_war("tensor", f"pj_{wkey}{par}")
                def mm_group(e, wsb=wsb, pj=pj, par=par, ec=ec, xf=xf):
                    for dk in range(ND):
                        ins = e.matmul(pj[par], wsb[:, dk, ec * 128:(ec + 1) * 128],
                                       xf[:, dk, :], start=(dk == 0), stop=(dk == ND - 1))
                    return ins
                tv = inc("tensor", mm_group, "sP")
                if wkey == "d":
                    t_pj_last = tv
                war.setdefault("_t_pj", {})
                war["_t_pj"][(wkey, par)] = tv
            tpj = war["_t_pj"]
            a_sg = a_tv = a_sa = 0
            for m in range(2):
                j = 2 * jj + m
                sl = slice(m * CH, (m + 1) * CH)
                wait("scalar", "sP", tpj[("g", par)])
                wait_war("scalar", f"sg{m}")
                a_sg = inc("scalar", lambda e, par=par, ec=ec, sl=sl, m=m: e.activation(
                    out=sg_scr2[:, m, :], in_=pj_g[par][:, sl], func=AF.Sigmoid,
                    bias=bias3[:, 0, ec:ec + 1]), "sA")
                wait("scalar", "sP", tpj[("v", par)])
                wait_war("scalar", f"tv{m}")
                a_tv = inc("scalar", lambda e, par=par, ec=ec, sl=sl, m=m: e.activation(
                    out=tv_scr2[:, m, :], in_=pj_v[par][:, sl], func=AF.Tanh,
                    bias=bias3[:, 1, ec:ec + 1]), "sA")
                ab = a_b if m == 0 else a_b2
                ep = ec % 2
                wait("scalar", "sP", tpj[("d", par)])
                wait_war("scalar", f"a_b{ep}m{m}")
                a_sa = inc("scalar", lambda e, par=par, ec=ec, sl=sl, ab=ab, ep=ep: e.activation(
                    out=ab[:, ep, :], in_=pj_d[par][:, sl], func=AF.Sigmoid,
                    bias=bias3[:, 2, ec:ec + 1]), "sA")
                wait("vector", "sA", a_tv)
                wait_war("vector", f"xs_b{ep}")
                v_xs = inc("vector", lambda e, ep=ep, m=m: e.tensor_mul(
                    xs_b[:, ep, :], sg_scr2[:, m, :], tv_scr2[:, m, :]), "sV")
                war[f"sg{m}"] = ("sV", v_xs)
                war[f"tv{m}"] = ("sV", v_xs)
                wait("vector", "sA", a_sa)
                v_a = inc("vector", lambda e, ep=ep, ab=ab: e.tensor_scalar(
                    out=ab[:, ep, :], in0=ab[:, ep, :], scalar1=0.998, scalar2=0.001,
                    op0=ALU.mult, op1=ALU.add), "sV")
                if j == 0:
                    h_init, c_init = 0.0, 1.0
                else:
                    h_init = h_all[:, ec, CH * j - 1:CH * j]
                    c_init = ca_f[:, ec, CH - 1:CH]
                v_h = inc("vector", lambda e, ec=ec, j=j, h_init=h_init, ab=ab, ep=ep: e.tensor_tensor_scan(
                    out=h_all[:, ec, CH * j:CH * (j + 1)], data0=ab[:, ep, :], data1=xs_b[:, ep, :],
                    initial=h_init, op0=ALU.mult, op1=ALU.add), "sV")
                war[f"xs_b{ep}"] = ("sV", v_h)
                v_ca = inc("vector", lambda e, ec=ec, j=j, c_init=c_init, ab=ab, ep=ep: e.tensor_tensor_scan(
                    out=ca_f[:, ec, :], data0=ab[:, ep, :], data1=zeros[:, :],
                    initial=c_init, op0=ALU.mult, op1=ALU.add), "sV")
                war[f"a_b{ep}m{m}"] = ("sV", v_ca)
                v_cc = inc("vector", lambda e, ec=ec, j=j: e.tensor_copy(
                    cumA[:, ec, CH * j:CH * (j + 1)], ca_f[:, ec, :]), "sV")
            war[f"pj_g{par}"] = ("sA", a_sg)
            war[f"pj_v{par}"] = ("sA", a_tv)
            war[f"pj_d{par}"] = ("sA", a_sa)
            if ec == 5 and jj + 1 < NCH // 2:
                v_xfm_g[jj + 1] = emit_group_input(jj + 1)
        war[f"xfm{jj % 2}"] = ("sP", t_pj_last)

    def finish_early():
        wait("sync", "sV", cnt["sV"])
        wait("sync", "sA", cnt["sA"])
        wait("sync", "sP", cnt["sP"])
        wait("sync", "sL", cnt["sL"])
        s_fin = inc("sync", lambda e: e.dma_start(
            out=bass.AP(outh, 0, [[D, 128], [128 * D, NT], [1, D]]),
            in_=out_all[:, :, :]), "sS", 16)
        wait("sync", "sS", s_fin)

    def emit_all():
        with (
            nc.semaphore("sL") as hL,
            nc.semaphore("sS") as hS,
            nc.semaphore("sP") as hP,
            nc.semaphore("sA") as hA,
            nc.semaphore("sV") as hV,
            nc.semaphore("sC") as hC,
            nc.gpsimd.register("rA") as rA,
            nc.gpsimd.register("rB") as rB,
            nc.sync.register("rS") as rS,
            nc.scalar.register("rC") as rC,
            nc.Block() as block,
        ):
            SEM.update(sL=hL, sS=hS, sP=hP, sA=hA, sV=hV, sC=hC)
            REGS.update(rA=rA, rB=rB, rS=rS, rC=rC)
            for eng in ENGINES:
                getattr(block, eng)(lambda e, lst=ops[eng]: [f(e) for f in lst])

    if stop == "M":
        finish_early(); emit_all(); return nc

    emit("vector", lambda e: e.tensor_copy(cc_sb[:, 0:ND], ca_f[:, :, CH - 1]))
    emit("vector", lambda e: e.drain())
    v_ccsb = inc_drain("vector", lambda e: e.tensor_copy(cc_sb[:, ND:2 * ND], h_all[:, :, T - 1]), "sV")
    wait("gpsimd", "sV", v_ccsb)
    l_cc1 = dma("gpsimd", bass.AP(cc_in, 0, [[16, 128], [1, 16]]), cc_sb[:, :])
    wait("gpsimd", "sL", l_cc1)
    c_cc = inc("gpsimd", lambda e: e.collective_compute(
        "AllGather", ALU.bypass, replica_groups=[list(range(NCORES))],
        ins=[cc_in.ap().opt()], outs=[cc_out.ap().opt()]), "sC", 1)
    wait("gpsimd", "sC", c_cc)
    l_al = dma("gpsimd", Aallr[:, :, :], bass.AP(cc_out, 0, [[16, 128], [16 * 128, NCORES], [1, ND]]))
    l_hl = dma("gpsimd", Hallr[:, :, :], bass.AP(cc_out, ND, [[16, 128], [16 * 128, NCORES], [1, ND]]))
    wait("vector", "sL", l_hl)
    emit("vector", lambda e: e.memset(c0_b[:, :], 0.0))
    emit("vector", lambda e: e.drain())
    emit("vector", lambda e: e.memset(carry[:, :], 0.0))
    emit("vector", lambda e: e.drain())
    for r in range(NCORES):
        src = c0_b if r % 2 == 0 else c1_b
        dst = c1_b if r % 2 == 0 else c0_b
        emit("vector", lambda e, r=r, src=src: e.scalar_tensor_tensor(
            out=carry[:, :], in0=src[:, :], scalar=csel[:, r:r + 1], in1=carry[:, :],
            op0=ALU.mult, op1=ALU.add))
        emit("vector", lambda e: e.drain())
        if r < NCORES - 1:
            emit("vector", lambda e, r=r, src=src, dst=dst: e.tensor_mul(
                dst[:, :], Aallr[:, r, :], src[:, :]))
            emit("vector", lambda e: e.drain())
            emit("vector", lambda e, r=r, dst=dst: e.tensor_add(
                dst[:, :], dst[:, :], Hallr[:, r, :]))
            emit("vector", lambda e: e.drain())
    v_corr = 0
    for ec in range(ND):
        v_corr = inc("vector", lambda e, ec=ec: e.scalar_tensor_tensor(
            out=h_all[:, ec, :], in0=cumA[:, ec, :], scalar=carry[:, ec:ec + 1],
            in1=h_all[:, ec, :], op0=ALU.mult, op1=ALU.add), "sV")

    if stop == "C":
        finish_early(); emit_all(); return nc

    hp = [psE, psF]
    v_res_last = 0
    for a in range(NT):
        tp = hp[a % 2]
        wait("tensor", "sV", v_corr)
        wait_war("tensor", f"hp{a % 2}")
        def htr_group(e, a=a, tp=tp):
            for ec in range(ND):
                ins = e.transpose(tp[:, ec * 128:(ec + 1) * 128],
                                  h_all[:, ec, a * 128:(a + 1) * 128], eye_f[:, :])
            return ins
        t_h = inc("tensor", htr_group, "sP")
        wait("vector", "sP", t_h)
        v_res = inc("vector", lambda e, a=a, tp=tp: e.tensor_copy(res1[:, a, :], tp[:, :]), "sV")
        war[f"hp{a % 2}"] = ("sV", v_res)
        v_res_last = v_res
    wait("gpsimd", "sV", v_res_last)
    l_resx = dma("gpsimd", res1[:, :, :],
                 bass.AP(xh, 0, [[D, 128], [128 * D, NT], [1, D]]), accum=True)
    a_sq = {}
    v_st = {}
    for a in range(NT):
        wait("vector", "sL", l_resx)
        v_st[a] = inc("vector", lambda e, a=a: e.reduce_sum(
            stc(0, a), res1[:, a, :], axis=AX.X), "sV")
        wait("scalar", "sL", l_resx)
        a_sq[a] = inc("scalar", lambda e, a=a: e.activation(
            out=sq_scr[:, :], in_=res1[:, a, :], func=AF.Square,
            accum_out=stc(1, a)), "sA")
    wait("vector", "sA", a_sq[NT - 1])
    emit("vector", lambda e: e.drain())
    emit("vector", lambda e: e.tensor_scalar_mul(st(2), st(0), 1.0 / D))
    emit("vector", lambda e: e.drain())
    emit("vector", lambda e: e.tensor_scalar_mul(st(3), st(1), 1.0 / D))
    emit("vector", lambda e: e.drain())
    emit("vector", lambda e: e.tensor_mul(st(4), st(2), st(2)))
    emit("vector", lambda e: e.drain())
    emit("vector", lambda e: e.tensor_sub(st(3), st(3), st(4)))
    emit("vector", lambda e: e.drain())
    v_var = inc_drain("vector", lambda e: e.tensor_scalar_add(st(3), st(3), 1e-5), "sV")
    wait("scalar", "sV", v_var)
    a_sd = inc_drain("scalar", lambda e: e.activation(
        out=st(4), in_=st(3), func=AF.Sqrt), "sA")
    wait("vector", "sA", a_sd)
    emit("vector", lambda e: e.reciprocal(st(5), st(4)))
    v_rstd = inc_drain("vector", lambda e: e.engine_nop(), "sV")
    v_y1fm_last = 0
    s_y1big = 0
    for a in range(NT):
        par = a % 2
        wait("vector", "sV", v_rstd)
        wait_war("vector", f"y1s{par}")
        v_y1 = inc("vector", lambda e, a=a, par=par: e.tensor_scalar(
            out=y1_scr[:, par, :], in0=res1[:, a, :], scalar1=stc(2, a),
            scalar2=stc(5, a), op0=ALU.subtract, op1=ALU.mult), "sV")
        wait("sync", "sV", v_y1)
        s_y1big = inc("sync", lambda e, a=a, par=par: e.dma_start(
            out=bass.AP(y_dram, a * 128 * D, [[D, 128], [1, D]]),
            in_=y1_scr[:, par, :]), "sS", 16)
        war[f"y1s{par}"] = ("sS", s_y1big)
        v_y1b = inc("vector", lambda e, a=a: e.tensor_scalar(
            out=y1b_tm[:, a, :], in0=res1[:, a, :], scalar1=stc(2, a),
            scalar2=stc(5, a), op0=ALU.subtract, op1=ALU.mult), "sV")
        tp = tr_ps[par]
        wait("tensor", "sV", v_y1b)
        wait_war("tensor", f"tr{par}")
        def ytr_group(e, a=a, tp=tp):
            for dc in range(ND):
                ins = e.transpose(tp[:, dc * 128:(dc + 1) * 128],
                                  y1b_tm[:, a, dc * 128:(dc + 1) * 128], eye_b[:, :])
            return ins
        t_y = inc("tensor", ytr_group, "sP")
        wait("vector", "sP", t_y)
        v_yfm = inc("vector", lambda e, a=a, tp=tp: e.tensor_copy(
            y1_fm[:, :, a * 128:(a + 1) * 128],
            tp[:, :].rearrange("p (k i) -> p k i", i=128)), "sV")
        war[f"tr{par}"] = ("sV", v_yfm)
        v_y1fm_last = v_yfm
        gps = [psC, psD]
        wait("tensor", "sV", v_yfm)
        wait("tensor", "sL", l_pre)
        wait_war("tensor", f"g{par}")
        def gate_group(e, a=a, par=par):
            for dk in range(ND):
                e.matmul(gps[par][:, 0:NE], y1_fm[:, dk, a * 128:(a + 1) * 128],
                         fwg_b[:, dk, :], start=(dk == 0), stop=False)
            return e.matmul(gps[par][:, 0:NE], ones_b[0:1, :], fbg_b[0:1, :],
                            start=False, stop=True)
        t_g = inc("tensor", gate_group, "sP")
        wait("vector", "sP", t_g)
        v_g = inc("vector", lambda e, a=a, par=par: e.tensor_copy(
            gl_all[:, a, :], gps[par][:, 0:NE]), "sV")
        war[f"g{par}"] = ("sV", v_g)

    if stop == "R":
        finish_early(); emit_all(); return nc


    emit("vector", lambda e: e.drain())
    emit("vector", lambda e: e.reduce_max(top1w[:, :].unsqueeze(2), gl_all[:, :, :], axis=AX.X))
    emit("vector", lambda e: e.drain())
    mx_b = top1w[:, :].unsqueeze(2).broadcast_to([128, NT, NE])
    emit("vector", lambda e: e.tensor_tensor(exf[:, :, :], gl_all[:, :, :], mx_b, ALU.subtract))
    v_sub = inc("vector", lambda e: e.tensor_tensor(
        eqf[:, :, :], gl_all[:, :, :], mx_b, ALU.is_ge), "sV")
    wait("scalar", "sV", v_sub)
    a_exp = inc("scalar", lambda e: e.activation(out=exf[:, :, :], in_=exf[:, :, :], func=AF.Exp), "sA")
    wait("vector", "sA", a_exp)
    emit("vector", lambda e: e.reduce_sum(st(0).unsqueeze(2), exf[:, :, :], axis=AX.X))
    emit("vector", lambda e: e.drain())
    emit("vector", lambda e: e.reciprocal(top1w[:, :], st(0)))
    emit("vector", lambda e: e.drain())
    ed_b = edesc[:, :].unsqueeze(1).broadcast_to([128, NT, NE])
    emit("vector", lambda e: e.tensor_tensor(valf[:, :, :], eqf[:, :, :], ed_b, ALU.mult))
    emit("vector", lambda e: e.drain())
    emit("vector", lambda e: e.reduce_max(st(0).unsqueeze(2), valf[:, :, :], axis=AX.X))
    emit("vector", lambda e: e.drain())
    vm_b = st(0).unsqueeze(2).broadcast_to([128, NT, NE])
    emit("vector", lambda e: e.tensor_tensor(gsel[:, :, :], valf[:, :, :], vm_b, ALU.is_equal))
    emit("vector", lambda e: e.drain())
    w_b = top1w[:, :].unsqueeze(2).broadcast_to([128, NT, NE])
    v_gsel = inc("vector", lambda e: e.tensor_tensor(
        gsel[:, :, :], gsel[:, :, :], w_b, ALU.mult), "sV")

    if stop == "G":
        finish_early(); emit_all(); return nc

    t_rt = 0
    for a in range(NT):
        tp = psE if a < 8 else psF
        wait("tensor", "sV", v_gsel)
        if a == 0:
            wait_war("tensor", "hp0")
        if a == 8:
            wait_war("tensor", "hp1")
        def rt_group(e, a=a, tp=tp):
            return e.transpose(tp[0:8, (a % 8) * 128:((a % 8) + 1) * 128],
                               gsel[:, a, :], eye_f[:, :])
        t_rt = inc("tensor", rt_group, "sP")
    wait("vector", "sP", t_rt)
    emit("vector", lambda e: e.tensor_copy(gfm[0:8, 0:1024], psE[0:8, :]))
    v_gfm = inc("vector", lambda e: e.tensor_copy(gfm[0:8, 1024:2048], psF[0:8, :]), "sV")
    emit("vector", lambda e: e.tensor_single_scalar(out=mfm[0:8, :], in_=gfm[0:8, :],
                                                    scalar=0.0, op=ALU.is_gt))
    emit("vector", lambda e: e.tensor_tensor_scan(
        out=cumslot[0:8, :], data0=mfm[0:8, :], data1=mfm[0:8, :],
        initial=0.0, op0=ALU.add, op1=ALU.max))
    emit("vector", lambda e: e.drain())
    emit("vector", lambda e: e.tensor_copy(counts_col[:, :], cumslot[0:8, T - 1:T]))
    emit("vector", lambda e: e.drain())
    v_counts = inc_drain("vector", lambda e: e.tensor_sub(cumslot[0:8, :], cumslot[0:8, :], mfm[0:8, :]), "sV")
    if stop == "RT1":
        finish_early(); emit_all(); return nc
    wait("tensor", "sV", v_counts)
    wait_war("tensor", "g0")
    t_cr = inc("tensor", lambda e: e.matmul(psC[0:1, 0:NE], counts_col[:, 0:1],
                                            eye_f[0:8, 0:NE], start=True, stop=True), "sP")
    wait("vector", "sP", t_cr)
    v_cr = inc_drain("vector", lambda e: e.tensor_copy(cr8[:, :], psC[0:1, 0:NE]), "sV")
    emit("vector", lambda e: e.tensor_scalar(out=iota8r[:, :], in0=edesc[0:1, :],
                                             scalar1=-1.0, scalar2=8.0, op0=ALU.mult, op1=ALU.add))
    emit("vector", lambda e: e.drain())
    emit("vector", lambda e: e.tensor_copy(rwork[:, :], cr8[:, :]))
    emit("vector", lambda e: e.memset(baser[:, :], 0.0))
    emit("vector", lambda e: e.drain())
    for k in range(NE):
        emit("vector", lambda e: e.reduce_max(rmk[:, :], rwork[:, :], axis=AX.X))
        emit("vector", lambda e: e.drain())
        emit("vector", lambda e: e.tensor_scalar(out=reqr[:, :], in0=rwork[:, :],
                                                 scalar1=rmk[0:1, 0:1], scalar2=None, op0=ALU.is_ge))
        emit("vector", lambda e: e.drain())
        emit("vector", lambda e: e.tensor_mul(rvalr[:, :], reqr[:, :], edesc[0:1, :]))
        emit("vector", lambda e: e.drain())
        emit("vector", lambda e: e.reduce_max(rvmr[:, :], rvalr[:, :], axis=AX.X))
        emit("vector", lambda e: e.drain())
        emit("vector", lambda e: e.tensor_scalar(out=rmskr[:, :], in0=rvalr[:, :],
                                                 scalar1=rvmr[0:1, 0:1], scalar2=None, op0=ALU.is_equal))
        emit("vector", lambda e: e.drain())
        emit("vector", lambda e: e.tensor_mul(rtmp8[:, :], rmskr[:, :], iota8r[:, :]))
        emit("vector", lambda e: e.drain())
        emit("vector", lambda e: e.reduce_sum(rtmp1[:, :], rtmp8[:, :], axis=AX.X))
        emit("vector", lambda e: e.drain())
        emit("vector", lambda e, k=k: e.tensor_copy(ekr[:, k:k + 1], rtmp1[:, :]))
        emit("vector", lambda e: e.drain())
        emit("vector", lambda e, k=k: e.scalar_tensor_tensor(
            out=baser[:, :], in0=rmskr[:, :], scalar=float(RBASE[k]), in1=baser[:, :],
            op0=ALU.mult, op1=ALU.add))
        emit("vector", lambda e: e.drain())
        emit("vector", lambda e: e.scalar_tensor_tensor(
            out=rwork[:, :], in0=rmskr[:, :], scalar=-1e9, in1=rwork[:, :],
            op0=ALU.mult, op1=ALU.add))
        emit("vector", lambda e: e.drain())
    emit("vector", lambda e: e.tensor_copy(ek_i[:, :], ekr[:, :]))
    v_rank = inc_drain("vector", lambda e: e.engine_nop(), "sV")
    if stop == "RT2":
        finish_early(); emit_all(); return nc
    wait("tensor", "sV", v_rank)
    wait_war("tensor", "g1")
    t_bc = inc("tensor", lambda e: e.matmul(psD[0:8, 0:1], baser[0:1, :], one1[0:1, 0:1],
                                            start=True, stop=True), "sP")
    wait("vector", "sP", t_bc)
    v_bc = inc_drain("vector", lambda e: e.tensor_copy(base_col[:, :], psD[0:8, 0:1]), "sV")
    emit("vector", lambda e: e.tensor_scalar_add(cumslot[0:8, :], cumslot[0:8, :], base_col[:, 0:1]))
    v_cm = inc("vector", lambda e: e.tensor_mul(cm8[0:8, :], cumslot[0:8, :], mfm[0:8, :]), "sV")
    wait("tensor", "sV", v_cm)
    wait("tensor", "sV", v_cr)
    wait("tensor", "sV", v_bc)
    def cl_group(e):
        for a in range(NT):
            ins = e.matmul(psC[:, a:a + 1], cm8[0:8, a * 128:(a + 1) * 128],
                           onesf8[:, 0:1], start=True, stop=True)
        return ins
    t_cl = inc("tensor", cl_group, "sP")
    wait("vector", "sP", t_cl)
    emit("vector", lambda e: e.tensor_copy(gslot_tm[:, :], psC[:, 0:NT]))
    v_gsr = inc_drain("vector", lambda e: e.engine_nop(), "sV")
    war["g0"] = ("sV", v_gsr)
    war["w2b"] = ("sP", cnt["sP"])
    emit("vector", lambda e: e.tensor_copy(gslot_i[:, :], gslot_tm[:, :]))
    v_gsi = inc_drain("vector", lambda e: e.engine_nop(), "sV")
    if stop == "RT3":
        finish_early(); emit_all(); return nc
    wait("sync", "sV", v_rank)
    for k in (() if "b1" in skip else range(NE)):
        def b1g(e, k=k):
            e.reg_load(REGS["rS"], ek_i[0:1, k:k + 1])
            e.reg_mul(REGS["rS"], REGS["rS"], NH)
            return e.dma_start(out=b1r[:, k, :],
                               in_=bass.AP(fb1h, REGS["rS"], [[NE * NH, 128], [1, NH]]))
        inc("sync", b1g, "sS", 16)

    def w1_stage(rank, par_s):
        def stg_s(e, rank=rank, par_s=par_s):
            e.reg_load(REGS["rS"], ek_i[0:1, rank:rank + 1])
            e.reg_mul(REGS["rS"], REGS["rS"], D * HID)
            return e.dma_start(out=bass.AP(w1s[par_s], 0, [[32768, 64], [8192, 4], [1, 8192]]),
                               in_=bass.AP(fw1h, REGS["rS"], [[32768, 64], [8192, 4], [1, 8192]]))
        def stg_c(e, rank=rank, par_s=par_s):
            e.reg_load(REGS["rC"], ek_i[0:1, rank:rank + 1])
            e.reg_mul(REGS["rC"], REGS["rC"], D * HID)
            e.reg_add(REGS["rC"], REGS["rC"], 64 * 32768)
            return e.dma_start(out=bass.AP(w1s[par_s], 64 * 32768, [[32768, 64], [8192, 4], [1, 8192]]),
                               in_=bass.AP(fw1h, REGS["rC"], [[32768, 64], [8192, 4], [1, 8192]]))
        inc("sync", stg_s, "sS", 16)
        return inc("scalar", stg_c, "sS", 16)

    wait("scalar", "sV", v_rank)
    s_stage = {0: (0 if "w1s" in skip else w1_stage(0, 0)),
               1: (0 if "w1s" in skip else w1_stage(1, 1))}
    wait("scalar", "sV", v_rank)
    s_w2s = {}

    def emit_w2stage(i, rk):
        def w2stage(e, rk=rk, i=i):
            e.reg_load(REGS["rC"], ek_i[0:1, rk:rk + 1])
            e.reg_mul(REGS["rC"], REGS["rC"], HID * D)
            return e.dma_start(out=bass.AP(w2s[i], 0, [[32768, 128], [8192, 4], [1, 8192]]),
                               in_=bass.AP(fw2h, REGS["rC"], [[32768, 128], [8192, 4], [1, 8192]]))
        s_w2s[rk] = inc("scalar", w2stage, "sS", 16)
    if stop == "RT4":
        finish_early(); emit_all(); return nc
    wait("gpsimd", "sV", v_gsi)
    wait("gpsimd", "sS", s_zf)
    l_ysc = 0
    for a in range(NT):
        l_ysc = inc("gpsimd", lambda e, a=a: e.indirect_dma_start(
            out=xg_dram.ap(), out_offset=bass.IndirectOffsetOnAxis(gslot_i[:, a:a + 1], 0),
            in_=y1b_tm[:, a, :].opt(), in_offset=None,
            bounds_check=NSLOT - 1, oob_is_err=False), "sL", 16)
    war["y1tm"] = ("sL", l_ysc)
    l_route_done = cnt["sL"]

    if stop == "RT":
        finish_early(); emit_all(); return nc

    p1ps = [psC, psD]
    p2ps = [psE, psF]
    nw2 = 0
    l_w2v = 0
    l_b2 = 0
    s_w1reads = {}
    for pi, (rank, sbase, cap, loadw) in enumerate(PASSES):
        ntp = cap // 128
        par_p = pi % 2
        if "w2s" not in skip:
            if pi == 4:
                emit_w2stage(0, NE - 2)
            elif pi == 5:
                emit_w2stage(1, NE - 1)
        wait("gpsimd", "sL", l_route_done)
        wait_war("gpsimd", "xgt")
        l_xg = dma("gpsimd", xgt[:, 0:ntp, :],
                   bass.AP(xg_dram, sbase * D, [[D, 128], [128 * D, ntp], [1, D]]))
        if loadw:
            nw2 += 1
        b2p = nw2 % 2
        v_xgfm = 0
        for t in range(ntp):
            tp = tr_ps[t % 2]
            wait("tensor", "sL", l_xg)
            wait_war("tensor", f"tr{t % 2}")
            def xtr_group(e, t=t, tp=tp):
                for dc in range(ND):
                    ins = e.transpose(tp[:, dc * 128:(dc + 1) * 128],
                                      xgt[:, t, dc * 128:(dc + 1) * 128], eye_b[:, :])
                return ins
            t_x = inc("tensor", xtr_group, "sP")
            wait("vector", "sP", t_x)
            wait_war("vector", "xg_fm")
            v_xgfm = inc("vector", lambda e, t=t, tp=tp: e.tensor_copy(
                xg_fm[:, :, t * 128:(t + 1) * 128],
                tp[:, :].rearrange("p (k i) -> p k i", i=128)), "sV")
            war[f"tr{t % 2}"] = ("sV", v_xgfm)
        war["xgt"] = ("sP", cnt["sP"])
        nbl = nblocks(cap)
        gi = 0
        t_p1_last = 0
        li_p = nw2 - 1
        wait("gpsimd", "sS", s_stage[li_p])
        for hc in range(ND):
            parw = hc % 2
            wait_war("gpsimd", f"w1b{parw}")
            l_w1 = dma("gpsimd", w1b[:, parw, :, :],
                       bass.AP(w1s[li_p % 3], hc * 4096,
                               [[32768, 128], [1, 4096]]))
            for hk in range(4):
                for (no, nw) in nbl:
                    pp = p1ps[gi % 2]
                    gi += 1
                    wait("tensor", "sL", l_w1)
                    wait("tensor", "sV", v_xgfm)
                    wait_war("tensor", f"p1_{(gi - 1) % 2}")
                    def p1_group(e, hc=hc, hk=hk, no=no, nw=nw, pp=pp, parw=parw):
                        for dk in range(ND):
                            ins = e.matmul(pp[:, 0:nw],
                                           w1b[:, parw, dk, hk * 128:(hk + 1) * 128],
                                           xg_fm[:, dk, no:no + nw],
                                           start=(dk == 0), stop=(dk == ND - 1))
                        return ins
                    t_p1 = inc("tensor", p1_group, "sP")
                    t_p1_last = t_p1
                    wait("scalar", "sP", t_p1)
                    wait_war("scalar", "p2_read")
                    wait_war("scalar", "y1tm")
                    a_h = inc("scalar", lambda e, rank=rank, hc=hc, hk=hk, no=no, nw=nw, pp=pp: e.activation(
                        out=h_half[:, hc * 4 + hk, no:no + nw],
                        in_=pp[:, 0:nw], func=AF.Silu,
                        bias=b1r[:, rank, hc * 4 + hk:hc * 4 + hk + 1]), "sA")
                    war[f"p1_{(gi - 1) % 2}"] = ("sA", a_h)
            war[f"w1b{parw}"] = ("sP", t_p1_last)
        war["xg_fm"] = ("sP", t_p1_last)
        a_p1_done = cnt["sA"]
        s_w1reads[li_p] = cnt["sL"]
        if loadw:
            wait("gpsimd", "sL", l_route_done)
            wait_war("gpsimd", "w2b")
            if rank >= NE - 2:
                wait("gpsimd", "sS", s_w2s[rank])
                l_w2v = dma("gpsimd", w2b[:, :, :],
                            bass.AP(w2s[rank - (NE - 2)], 0, [[32768, 128], [8192, 4], [1, 8192]]))
            else:
                def w2g(e, rank=rank):
                    e.reg_load(REGS["rA"], ek_i[0:1, rank:rank + 1])
                    e.reg_mul(REGS["rA"], REGS["rA"], HID * D)
                    return e.dma_start(out=w2b[:, :, :],
                                       in_=bass.AP(fw2h, REGS["rA"], [[32768, 128], [8192, 4], [1, 8192]]))
                l_w2v = inc("gpsimd", w2g, "sL", 16)
            def b2g(e, rank=rank, b2p=b2p):
                e.reg_load(REGS["rS"], ek_i[0:1, rank:rank + 1])
                e.reg_mul(REGS["rS"], REGS["rS"], D)
                return e.dma_start(out=b2f_cur[:, b2p, :],
                                   in_=bass.AP(fb2h, REGS["rS"], [[1, 1], [1, D]]))
            s_b2 = inc("sync", b2g, "sS", 16)
            wait("vector", "sS", s_b2)
            wait_war("vector", f"b2c{b2p}")
            l_b2 = inc("vector", lambda e, b2p=b2p: e.tensor_copy(
                b2_cur[:, b2p, :], b2f_cur[:, b2p, :]), "sV")
            li = nw2 - 1
            if li + 1 < NE and li + 1 not in s_stage:
                if li - 2 >= 0:
                    wait("sync", "sL", s_w1reads.get(li - 2, 0))
                    wait("scalar", "sL", s_w1reads.get(li - 2, 0))
                s_stage[li + 1] = w1_stage(li + 1, (li + 1) % 3)
        wait("tensor", "sA", a_p1_done)
        wait("tensor", "sV", l_b2)
        wait("tensor", "sL", l_w2v)
        for t in range(ntp):
            pp = p2ps[t % 2]
            wait_war("tensor", f"p2_{t % 2}")
            def p2_group(e, t=t, pp=pp, b2p=nw2 % 2):
                for hk in range(NH):
                    e.matmul(pp[:, 0:512], h_half[:, hk, t * 128:(t + 1) * 128],
                             w2b[:, hk, 0:512], start=(hk == 0), stop=False)
                    e.matmul(pp[:, 512:1024], h_half[:, hk, t * 128:(t + 1) * 128],
                             w2b[:, hk, 512:1024], start=(hk == 0), stop=False)
                e.matmul(pp[:, 0:512], ones_b[0:1, :], b2_cur[0:1, b2p, 0:512],
                         start=False, stop=True)
                return e.matmul(pp[:, 512:1024], ones_b[0:1, :], b2_cur[0:1, b2p, 512:1024],
                                start=False, stop=True)
            t_p2 = inc("tensor", p2_group, "sP")
            par_o = t % 4
            wait("scalar", "sP", t_p2)
            wait_war("scalar", f"o_scr{par_o}")
            a_o = inc("scalar", lambda e, pp=pp, par_o=par_o: e.activation(
                out=o_scr[:, par_o, :], in_=pp[:, :], func=AF.Identity), "sA")
            war[f"p2_{t % 2}"] = ("sA", a_o)
            wait("sync", "sA", a_o)
            s_o = inc("sync", lambda e, par_o=par_o, row0=sbase + t * 128: e.dma_start(
                out=bass.AP(oslot_dram, row0 * D, [[D, 128], [1, D]]),
                in_=o_scr[:, par_o, :]), "sS", 16)
            war[f"o_scr{par_o}"] = ("sS", s_o)
        war["p2_read"] = ("sP", cnt["sP"])
        war["w2b"] = ("sP", cnt["sP"])
        war[f"b2c{nw2 % 2}"] = ("sP", cnt["sP"])
    s_oslot_done = cnt["sS"]
    l_acc_done = cnt["sL"]
    s_y1_done = cnt["sS"]

    if stop == "F":
        finish_early(); emit_all(); return nc

    wait("sync", "sS", s_y1big)
    s_yo = inc("sync", lambda e: e.dma_start(
        out=yo_all[:, :, :], in_=bass.AP(y_dram, 0, [[D, 128], [128 * D, NT], [1, D]])), "sS", 16)
    a_sq2 = 0
    wait("gpsimd", "sS", s_oslot_done)
    for a in range(NT):
        par = a % 4
        wait_war("gpsimd", f"og{par}")
        l_og = inc("gpsimd", lambda e, a=a, par=par: e.indirect_dma_start(
            out=o_scr[:, par, :].opt(), out_offset=None,
            in_=oslot_dram.ap(), in_offset=bass.IndirectOffsetOnAxis(gslot_i[:, a:a + 1], 0),
            bounds_check=NSLOT - 1, oob_is_err=False), "sL", 16)
        wait("vector", "sL", l_og)
        wait("vector", "sS", s_yo)
        v_y = inc("vector", lambda e, a=a, par=par: e.scalar_tensor_tensor(
            out=yo_all[:, a, :], in0=o_scr[:, par, :], scalar=top1w[:, a:a + 1],
            in1=yo_all[:, a, :], op0=ALU.mult, op1=ALU.add), "sV")
        war[f"og{par}"] = ("sV", v_y)
        wait("vector", "sV", v_y)
        inc("vector", lambda e, a=a: e.reduce_sum(
            stc(0, a), yo_all[:, a, :], axis=AX.X), "sV")
        wait("scalar", "sV", v_y)
        a_sq2 = inc("scalar", lambda e, a=a: e.activation(
            out=sq_scr[:, :], in_=yo_all[:, a, :], func=AF.Square,
            accum_out=stc(1, a)), "sA")
    wait("vector", "sA", a_sq2)
    emit("vector", lambda e: e.drain())
    emit("vector", lambda e: e.tensor_scalar_mul(st(2), st(0), 1.0 / D))
    emit("vector", lambda e: e.drain())
    emit("vector", lambda e: e.tensor_scalar_mul(st(3), st(1), 1.0 / D))
    emit("vector", lambda e: e.drain())
    emit("vector", lambda e: e.tensor_mul(st(4), st(2), st(2)))
    emit("vector", lambda e: e.drain())
    emit("vector", lambda e: e.tensor_sub(st(3), st(3), st(4)))
    emit("vector", lambda e: e.drain())
    v_var2 = inc_drain("vector", lambda e: e.tensor_scalar_add(st(3), st(3), 1e-5), "sV")
    wait("scalar", "sV", v_var2)
    a_sd2 = inc_drain("scalar", lambda e: e.activation(
        out=st(4), in_=st(3), func=AF.Sqrt), "sA")
    wait("vector", "sA", a_sd2)
    emit("vector", lambda e: e.reciprocal(st(5), st(4)))
    v_rstd2 = inc_drain("vector", lambda e: e.engine_nop(), "sV")
    s_out = 0
    for a in range(NT):
        wait("vector", "sV", v_rstd2)
        v_out = inc("vector", lambda e, a=a: e.tensor_scalar(
            out=out_all[:, a, :], in0=yo_all[:, a, :], scalar1=stc(2, a),
            scalar2=stc(5, a), op0=ALU.subtract, op1=ALU.mult), "sV")
        wait("sync", "sV", v_out)
        s_out = inc("sync", lambda e, a=a: e.dma_start(
            out=bass.AP(outh, a * 128 * D, [[D, 128], [1, D]]),
            in_=out_all[:, a, :]), "sS", 16)
    wait("sync", "sS", s_out)

    emit_all()
    return nc


_NC_CACHE = None


def kernel(L0_attn_x, L0_attn_Wg, L0_attn_bg, L0_attn_Wv, L0_attn_bv, L0_attn_Wd,
           L0_attn_bd, L0_ffn_Wg, L0_ffn_bg, L0_ffn_W1, L0_ffn_b1, L0_ffn_W2, L0_ffn_b2):
    global _NC_CACHE
    x = np.ascontiguousarray(np.asarray(L0_attn_x, dtype=np.float32).reshape(16384, 1024))
    wg = np.ascontiguousarray(np.asarray(L0_attn_Wg, np.float32))
    wv = np.ascontiguousarray(np.asarray(L0_attn_Wv, np.float32))
    wd = np.ascontiguousarray(np.asarray(L0_attn_Wd, np.float32))
    bias3 = np.ascontiguousarray(
        np.stack([np.asarray(b, np.float32).reshape(8, 128).T for b in
                  (L0_attn_bg, L0_attn_bv, L0_attn_bd)], axis=1))
    fwg = np.ascontiguousarray(np.asarray(L0_ffn_Wg, np.float32))
    fbg = np.ascontiguousarray(np.asarray(L0_ffn_bg, np.float32).reshape(1, 8))
    fw1_raw = np.asarray(L0_ffn_W1, np.float32)
    fw1 = np.ascontiguousarray(
        fw1_raw.reshape(8, 8, 128, 8, 512).transpose(0, 2, 3, 1, 4).reshape(8, 1024, 4096))
    fb1c = np.ascontiguousarray(np.asarray(L0_ffn_b1, np.float32).reshape(8, 32, 128).transpose(2, 0, 1))
    fw2_raw = np.asarray(L0_ffn_W2, np.float32)
    fw2 = np.ascontiguousarray(
        fw2_raw.reshape(8, 32, 128, 1024).transpose(0, 2, 1, 3).reshape(8, 4096, 1024))
    fb2r = np.ascontiguousarray(np.asarray(L0_ffn_b2, np.float32))
    eye = np.eye(128, dtype=np.float32)
    ones1 = np.ones((1, 128), np.float32)
    edesc = np.tile(np.arange(8, 0, -1, dtype=np.float32)[None, :], (128, 1))
    iotatm = (np.arange(16, dtype=np.int32)[None, :] * 128 +
              np.arange(128, dtype=np.int32)[:, None]).astype(np.int32)

    import os as _os
    if _NC_CACHE is None:
        _NC_CACHE = build_nc(stop=_os.environ.get("KSTOP", "ALL"),
                             skip=tuple(_os.environ.get("KSKIP", "").split(",")))
    nc = _NC_CACHE

    in_maps = []
    for c in range(NCORES):
        csel = np.zeros((128, 8), np.float32)
        csel[:, c] = 1.0
        in_maps.append(dict(
            x=np.ascontiguousarray(x[c * 2048:(c + 1) * 2048]),
            wg=wg, wv=wv, wd=wd, bias3=bias3, fwg=fwg, fbg=fbg, fw1=fw1,
            fb1c=fb1c, fw2=fw2, fb2r=fb2r, eye=eye, ones1=ones1,
            csel=csel, edesc=edesc, iotatm=iotatm,
        ))
    res = run_bass_kernel_spmd(nc, in_maps, core_ids=list(range(NCORES)))
    out = np.concatenate([res.results[c]["out"] for c in range(NCORES)], axis=0)
    return out.reshape(1, 16384, 1024)
